# version 04
# baseline (speedup 1.0000x reference)
import numpy as np
import ml_dtypes
from contextlib import ExitStack

import concourse.bass as bass
import concourse.mybir as mybir
from concourse.bass_utils import run_bass_kernel_spmd

F32 = mybir.dt.float32
BF16 = mybir.dt.bfloat16
AF = mybir.ActivationFunctionType
OP = mybir.AluOpType
AX = mybir.AxisListType.X

D = 4096
HID = 11008
KVD = 1024
NH = 8
R = 16
B = 4
S = 4096
NCORE = 8
TOK = 2048
TB = 512
NTB = TOK // TB
TC = TB // 128
NDC = D // 128
NOC = HID // 128
OWS = [384] * 28 + [256]
OWB = [0]
for _w in OWS:
    OWB.append(OWB[-1] + _w)
DMW = 512
NDMW = D // DMW
NKC = S // 128
NKB = S // 512
EPS = 1e-5
SM_SCALE = 1.0 / float(np.sqrt(128.0))
SM_BIAS = -12.0

ENGS = ("pe", "act", "dve", "pool", "sp")
SEM_NAMES = ["c_pe", "c_act", "c_dve", "c_pool", "q_sp", "q_pool", "cc",
             "s_ln", "s_sl1", "s_sl3", "s_bsl", "s_w2a", "s_w2b",
             "s_xra", "s_xrb", "s_sta", "s_stb", "s_b2",
             "s_wqa", "s_wqb", "s_kv", "s_kst", "s_kt", "s_vh", "s_wo",
             "s_pna", "s_pnb", "s_psa", "s_psb"] + [f"s_w2p{i}" for i in range(8)]
SNAP_SEMS = [n for n in SEM_NAMES if n != "cc"]

def configure(tok, seq):
    global TOK, S, NTB, NKC, NKB
    TOK = tok
    S = seq
    NTB = TOK // TB
    NKC = S // 128
    NKB = S // 512


class Plan:

    def __init__(self, nc, semh):
        self.nc = nc
        self.semh = semh
        self.items = {e: [] for e in ENGS}
        self.cnt = {name: 0 for name in semh}

    def snapw(self):
        return [(s, self.cnt[s]) for s in SNAP_SEMS if self.cnt[s] > 0]

    def wait(self, eng, sem, val):
        if val <= 0:
            return
        assert val <= self.cnt[sem], \
            f"wait on future count: {sem} {val} > {self.cnt[sem]}"
        self.items[eng].append(("w", sem, val))

    def waitl(self, eng, waits):
        for (s, v) in waits:
            self.wait(eng, s, v)

    def op(self, eng, fn, inc=None, inc_k=1):
        n = None
        if inc is not None:
            self.cnt[inc] += inc_k
            n = self.cnt[inc]
        self.items[eng].append(("o", fn, inc, inc_k))
        return n

    def eop(self, eng, fn, waits=()):
        ch = "c_" + eng
        self.waitl(eng, waits)
        self.wait(eng, ch, self.cnt[ch])
        return self.op(eng, fn, inc=ch)

    def dma(self, q, out, in_, waits=(), sem=None):
        if sem is None:
            sem = "q_sp" if q == "sp" else "q_pool"
        self.waitl(q, waits)

        def fn(h, out=out, in_=in_):
            return h.dma_start(out=out, in_=in_)
        return self.op(q, fn, inc=sem, inc_k=16)

    def barrier(self):
        snap = {s: self.cnt[s] for s in self.cnt}
        for e in ENGS:
            for s, v in snap.items():
                if v > 0:
                    self.items[e].append(("w", s, v))

    def play(self):
        nc = self.nc

        def runner(eng):
            items = self.items[eng]
            semh = self.semh

            def go(h):
                for it in items:
                    if it[0] == "w":
                        h.wait_ge(semh[it[1]], it[2])
                    else:
                        _, fn, inc, k = it
                        ins = fn(h)
                        if inc is not None:
                            ins.then_inc(semh[inc], k)
            return go

        with nc.Block() as block:
            block.tensor(runner("pe"))
            block.scalar(runner("act"))
            block.vector(runner("dve"))
            block.gpsimd(runner("pool"))
            block.sync(runner("sp"))
        self.items = {e: [] for e in ENGS}


def mm(P, out_ap, lhsT_ap, rhs_ap, start, stop, waits=(), inc=None):
    P.waitl("pe", waits)

    def fn(h, o=out_ap, l=lhsT_ap, r=rhs_ap, st=start, sp=stop):
        return h.matmul(o, l, r, start=st, stop=sp)
    return P.op("pe", fn, inc=inc)


def transpose(P, out_ap, in_ap, ident_ap, waits=(), inc=None):
    P.waitl("pe", waits)

    def fn(h, o=out_ap, i=in_ap, d=ident_ap):
        return h.transpose(o, i, d)
    return P.op("pe", fn, inc=inc)


def cp_dve(P, out_ap, in_ap, waits=()):
    return P.eop("dve", (lambda h, o=out_ap, i=in_ap: h.tensor_copy(o, i)),
                 waits=waits)


def cp_act(P, out_ap, in_ap, waits=()):
    return P.eop("act", (lambda h, o=out_ap, i=in_ap: h.copy(o, i)),
                 waits=waits)


def emit_ln_tc(P, src_row_ap, xbuf, xnb, stats, eps_ap, xbuf_free, xnb_free,
               stats_free):
    nx = P.dma("sp", xbuf[:, :], src_row_ap, waits=xbuf_free, sem="s_ln")
    P.eop("dve", (lambda h, o=stats[:, 0:1], i=xbuf[:, :]:
                  h.reduce_sum(o, i, axis=AX)),
          waits=[("s_ln", nx)] + stats_free)
    nsq = P.eop("act", (lambda h, o=xnb[:, :], i=xbuf[:, :], a=stats[:, 3:4]:
                        h.activation(o, i, AF.Square, accum_out=a)),
                waits=[("s_ln", nx)] + xnb_free + stats_free)
    P.eop("dve", (lambda h, o=stats[:, 1:2], i=stats[:, 0:1]:
                  h.tensor_scalar_mul(o, i, -1.0 / D)))
    P.eop("dve", (lambda h, o=stats[:, 2:3], i=stats[:, 1:2]:
                  h.tensor_mul(o, i, i)))
    nvar = P.eop("dve", (lambda h, o=stats[:, 4:5], i=stats[:, 3:4],
                         m=stats[:, 2:3]:
                         h.scalar_tensor_tensor(o, i, 1.0 / D, m,
                                                OP.mult, OP.subtract)),
                 waits=[("c_act", nsq)])
    nstd = P.eop("act", (lambda h, o=stats[:, 5:6], i=stats[:, 4:5], b_=eps_ap:
                         h.activation(o, i, AF.Sqrt, bias=b_)),
                 waits=[("c_dve", nvar)])
    P.eop("dve", (lambda h, o=stats[:, 6:7], i=stats[:, 5:6]:
                  h.reciprocal(o, i)),
          waits=[("c_act", nstd)])
    nxn = P.eop("dve", (lambda h, o=xnb[:, :], i=xbuf[:, :],
                        s1=stats[:, 1:2], s2=stats[:, 6:7]:
                        h.tensor_scalar(o, i, s1, s2, OP.add, OP.mult)),
                waits=[("c_act", nsq)])
    return nx, nsq, nxn


def emit_transpose_fm(P, ident, banks, bankfree, xnb, nxn, xnT_t, tc):
    lastpe = None
    for gidx, g0 in enumerate(range(0, NDC, 8)):
        bi = gidx % 2
        bk = banks[bi]
        for j in range(8):
            dc = g0 + j
            w = []
            if j == 0:
                w = [("c_dve", nxn)] + bankfree[bi]
            lastpe = transpose(P, bk[:, j * 128:(j + 1) * 128],
                               xnb[:, dc * 128:(dc + 1) * 128], ident[:, :],
                               waits=w, inc="c_pe" if j == 7 else None)
        nev = cp_dve(P, xnT_t[:, g0:g0 + 8, tc * 128:(tc + 1) * 128],
                     bk[:, :].rearrange("p (j o) -> p j o", o=128),
                     waits=[("c_pe", lastpe)])
        bankfree[bi] = [("c_dve", nev)]
    return lastpe


def build_program():
    nc = bass.Bass("TRN2", target_bir_lowering=False)

    x_in = nc.dram_tensor("x_in", [TOK, D], F32, kind="ExternalInput").ap()
    wq = nc.dram_tensor("wq", [D, D], F32, kind="ExternalInput").ap()
    wk = nc.dram_tensor("wk", [D, KVD], F32, kind="ExternalInput").ap()
    wv = nc.dram_tensor("wv", [D, KVD], F32, kind="ExternalInput").ap()
    wo = nc.dram_tensor("wo", [KVD, D], F32, kind="ExternalInput").ap()
    w1 = nc.dram_tensor("w1", [HID, D], F32, kind="ExternalInput").ap()
    w3 = nc.dram_tensor("w3", [HID, D], F32, kind="ExternalInput").ap()
    w2 = nc.dram_tensor("w2", [D, HID], F32, kind="ExternalInput").ap()
    a1 = nc.dram_tensor("a1", [R, D], F32, kind="ExternalInput").ap()
    b1 = nc.dram_tensor("b1", [HID, R], F32, kind="ExternalInput").ap()
    a3 = nc.dram_tensor("a3", [R, D], F32, kind="ExternalInput").ap()
    b3 = nc.dram_tensor("b3", [HID, R], F32, kind="ExternalInput").ap()
    a2 = nc.dram_tensor("a2", [R, HID], F32, kind="ExternalInput").ap()
    b2 = nc.dram_tensor("b2", [D, R], F32, kind="ExternalInput").ap()
    identw = nc.dram_tensor("identw", [128, 128], BF16, kind="ExternalInput").ap()
    consts = nc.dram_tensor("consts", [128, 4], F32, kind="ExternalInput").ap()
    y_out = nc.dram_tensor("y_out", [TOK, D], F32, kind="ExternalOutput").ap()

    x1h = nc.dram_tensor("x1h", [TOK, D], F32, kind="Internal").ap()
    x2h = nc.dram_tensor("x2h", [TOK, D], F32, kind="Internal").ap()
    w1t_s = nc.dram_tensor("w1t_s", [D * HID], BF16, kind="Internal").ap()
    w3t_s = nc.dram_tensor("w3t_s", [D * HID], BF16, kind="Internal").ap()
    w2t_s = nc.dram_tensor("w2t_s", [HID, D], BF16, kind="Internal").ap()
    wqe_s = nc.dram_tensor("wqe_s", [D, KVD], BF16, kind="Internal").ap()
    b1t_s = nc.dram_tensor("b1t_s", [R, HID], BF16, kind="Internal").ap()
    b3t_s = nc.dram_tensor("b3t_s", [R, HID], BF16, kind="Internal").ap()
    b2t_s = nc.dram_tensor("b2t_s", [R, D], BF16, kind="Internal").ap()
    kx_own = nc.dram_tensor("kx_own", [TOK, 2 * KVD], BF16, kind="Internal").ap()
    kx_all = nc.dram_tensor("kx_all", [S, 2 * KVD], BF16, kind="Internal").ap()

    es_top = ExitStack()
    semh = {name: es_top.enter_context(nc.semaphore(name)) for name in SEM_NAMES}

    ident = es_top.enter_context(nc.sbuf_tensor("ident", [128, 128], BF16))
    csb = es_top.enter_context(nc.sbuf_tensor("csb", [128, 4], F32))
    a1t = es_top.enter_context(nc.sbuf_tensor("a1t", [128, NDC * R], BF16))
    a3t = es_top.enter_context(nc.sbuf_tensor("a3t", [128, NDC * R], BF16))
    a2t = es_top.enter_context(nc.sbuf_tensor("a2t", [128, NOC * R], BF16))

    P = Plan(nc, semh)
    EPS_AP = csb[:, 0:1]
    SMB_AP = csb[:, 1:2]
    SMS_AP = csb[:, 2:3]

    with ExitStack() as es:
        natb = es.enter_context(nc.sbuf_tensor("p0nat", [128, NOC * R], BF16))
        stage = es.enter_context(nc.sbuf_tensor("p0stg", [16, NOC * 128], BF16))
        anat = es.enter_context(nc.sbuf_tensor("p0anat", [16, NOC * 128], BF16))
        trp_ = es.enter_context(nc.psum_tensor("p0trp", [128, 1024], BF16))
        trp2_ = es.enter_context(nc.psum_tensor("p0trp2", [128, 1024], BF16))
        banks = [trp_, trp2_]
        bankfree = [[], []]

        P.dma("sp", ident[:, :], identw)
        P.dma("sp", csb[:, :], consts)
        P.wait("pe", "q_sp", P.cnt["q_sp"])

        def prep_at(a_src, dst, nchunk):
            nld = P.dma("pool", anat[0:16, 0:nchunk * 128], a_src,
                        waits=[("c_pe", P.cnt["c_pe"])])
            for gidx, g0 in enumerate(range(0, nchunk, 8)):
                gsz = min(8, nchunk - g0)
                bi = gidx % 2
                bk = banks[bi]
                lastpe = None
                for j in range(gsz):
                    c = g0 + j
                    w = []
                    if j == 0:
                        w = [("q_pool", nld)] + bankfree[bi]
                    lastpe = transpose(P, bk[:, j * 16:(j + 1) * 16],
                                       anat[0:16, c * 128:(c + 1) * 128],
                                       ident[0:16, 0:16],
                                       waits=w, inc="c_pe" if j == gsz - 1 else None)
                nev = cp_dve(P, dst[:, g0 * R:(g0 + gsz) * R],
                             bk[:, 0:gsz * R], waits=[("c_pe", lastpe)])
                bankfree[bi] = [("c_dve", nev)]

        prep_at(a1, a1t, NDC)
        prep_at(a3, a3t, NDC)
        prep_at(a2, a2t, NOC)

        def prep_bt(b_src, dst_dram, nchunk):
            src_v = b_src.rearrange("(c p) r -> p c r", p=128)
            nld = P.dma("pool",
                        natb[:, 0:nchunk * R].rearrange("p (c r) -> p c r", r=R),
                        src_v,
                        waits=[("c_pe", P.cnt["c_pe"]),
                               ("q_sp", P.cnt["q_sp"])])
            nst = None
            for gidx, g0 in enumerate(range(0, nchunk, 8)):
                gsz = min(8, nchunk - g0)
                bi = gidx % 2
                bk = banks[bi]
                lastpe = None
                for j in range(gsz):
                    c = g0 + j
                    w = []
                    if j == 0:
                        w = [("q_pool", nld)] + bankfree[bi]
                        if gidx in (0, 1) and nst is not None:
                            w.append(("q_sp", nst))
                    lastpe = transpose(P, bk[0:16, j * 128:(j + 1) * 128],
                                       natb[:, c * R:(c + 1) * R], ident[:, :],
                                       waits=w, inc="c_pe" if j == gsz - 1 else None)
                nev = P.eop("dve", (lambda h,
                                    o=stage[0:16, g0 * 128:(g0 + gsz) * 128],
                                    i=bk[0:16, 0:gsz * 128]:
                                    h.tensor_scalar_mul(o, i, 2.0)),
                            waits=[("c_pe", lastpe), ("q_sp", P.cnt["q_sp"])])
                bankfree[bi] = [("c_dve", nev)]
            nst = P.dma("sp", dst_dram, stage[0:16, 0:nchunk * 128],
                        waits=[("c_dve", P.cnt["c_dve"])])
            return nst

        prep_bt(b1, b1t_s[0:R, :], NOC)
        prep_bt(b3, b3t_s[0:R, :], NOC)
        prep_bt(b2, b2t_s[0:R, :], NDC)

        P.barrier()
        P.play()

    w1_v = w1.rearrange("(r p) d -> p r d", p=128)
    w3_v = w3.rearrange("(r p) d -> p r d", p=128)
    w2t_v2 = w2t_s.rearrange("(oc p) d -> p oc d", p=128)
    W2P_SEMS = [f"s_w2p{i}" for i in range(8)]

    def ffn(src, dst, tag, fuse_prep=False):
        src_v = src.rearrange("(tt p) d -> p tt d", p=128)
        dst_v = dst.rearrange("(tt p) d -> p tt d", p=128)
        with ExitStack() as es_f:
            gT = es_f.enter_context(nc.sbuf_tensor(tag + "gT", [128, NOC * TB], BF16))
            xr1t = es_f.enter_context(nc.sbuf_tensor(tag + "xr1t", [16, TB], BF16))
            xr3t = es_f.enter_context(nc.sbuf_tensor(tag + "xr3t", [16, TB], BF16))
            xr2t = es_f.enter_context(nc.sbuf_tensor(tag + "xr2t", [16, TB], BF16))
            xrtm = es_f.enter_context(nc.sbuf_tensor(tag + "xrtm", [128, 64], BF16))
            gT_t = gT.rearrange("p (oc t) -> p oc t", t=TB)

            for tb in range(NTB):
                t0 = tb * TB
                sfx = tag + str(tb)
                prep_a = fuse_prep and tb == 0
                prep_b = fuse_prep and tb == 0
                prep_q = fuse_prep and tb == 1
                with ExitStack() as es_a:
                    xnT = es_a.enter_context(
                        nc.sbuf_tensor(sfx + "xnT", [128, NDC * TB], BF16))
                    stats = es_a.enter_context(
                        nc.sbuf_tensor(sfx + "stats", [128, 8], F32))
                    slw1 = es_a.enter_context(
                        nc.sbuf_tensor(sfx + "slw1", [128, NDC * 384], BF16))
                    slw3 = es_a.enter_context(
                        nc.sbuf_tensor(sfx + "slw3", [128, NDC * 384], BF16))
                    s1b = es_a.enter_context(
                        nc.sbuf_tensor(sfx + "s1b", [128, 3 * 512], BF16))
                    bsl = es_a.enter_context(
                        nc.sbuf_tensor(sfx + "bsl", [16, 2 * 384], BF16))
                    trp_ = es_a.enter_context(
                        nc.psum_tensor(sfx + "trp", [128, 1024], BF16))
                    trp2_ = es_a.enter_context(
                        nc.psum_tensor(sfx + "trp2", [128, 1024], BF16))

                    banks = [trp_, trp2_]
                    bankfree = [P.snapw(), P.snapw()]
                    xnT_t = xnT.rearrange("p (dc t) -> p dc t", t=TB)
                    xr_free = [P.snapw()]

                    with ExitStack() as es_ln:
                        xbuf = es_ln.enter_context(
                            nc.sbuf_tensor(sfx + "xbuf", [128, D], F32))
                        xnb = es_ln.enter_context(
                            nc.sbuf_tensor(sfx + "xnb", [128, D], BF16))
                        xrp = es_ln.enter_context(
                            nc.psum_tensor(sfx + "xrp", [128, 512], F32))
                        xbuf_free = P.snapw()
                        xnb_free = P.snapw()
                        stats_free = P.snapw()
                        for tc in range(TC):
                            nx, nsq, nxn = emit_ln_tc(
                                P, src_v[:, tb * TC + tc, :], xbuf, xnb, stats,
                                EPS_AP, xbuf_free, xnb_free, stats_free)
                            lastpe = emit_transpose_fm(P, ident, banks, bankfree,
                                                       xnb, nxn, xnT_t, tc)
                            xbuf_free = [("c_dve", nxn), ("c_act", nsq)]
                            xnb_free = [("c_pe", lastpe)]
                            stats_free = [("c_dve", nxn)]
                        xnT_ready = ("c_dve", P.cnt["c_dve"])

                        def xr_pass(atile, outT, first_waits):
                            for tc in range(TC):
                                for dc in range(NDC):
                                    w = []
                                    if tc == 0 and dc == 0:
                                        w = list(first_waits) + xr_free[0]
                                    mm(P, xrp[:, tc * 16:(tc + 1) * 16],
                                       xnT_t[:, dc, tc * 128:(tc + 1) * 128],
                                       atile[:, dc * R:(dc + 1) * R],
                                       start=(dc == 0), stop=(dc == NDC - 1),
                                       waits=w,
                                       inc="c_pe" if (dc == NDC - 1) else None)
                            npe = P.cnt["c_pe"]
                            nev = cp_dve(P, xrtm[:, 0:64], xrp[:, 0:64],
                                         waits=[("c_pe", npe)] + xr_free[0])
                            lastpe = None
                            for tc in range(TC):
                                w = []
                                if tc == 0:
                                    w = [("c_dve", nev)] + bankfree[0]
                                lastpe = transpose(
                                    P, banks[0][0:16, tc * 128:(tc + 1) * 128],
                                    xrtm[:, tc * 16:(tc + 1) * 16], ident[:, :],
                                    waits=w, inc="c_pe" if tc == TC - 1 else None)
                            nev2 = cp_dve(P, outT[0:16, :], banks[0][0:16, 0:TB],
                                          waits=[("c_pe", lastpe)])
                            bankfree[0] = [("c_dve", nev2)]
                            xr_free[0] = [("c_dve", nev2), ("c_pe", lastpe)]
                            return ("c_dve", nev2)

                        xr1_ready = xr_pass(a1t, xr1t, [xnT_ready])
                        xr3_ready = xr_pass(a3t, xr3t, [xnT_ready])

                    h1ps = [es_a.enter_context(
                        nc.psum_tensor(sfx + f"h1ps{i}", [128, 512], F32))
                        for i in range(3)]
                    h3ps = [es_a.enter_context(
                        nc.psum_tensor(sfx + f"h3ps{i}", [128, 512], F32))
                        for i in range(3)]
                    natw = None
                    if prep_a:
                        natw = es_a.enter_context(
                            nc.sbuf_tensor(sfx + "natw", [128, 8192], BF16))

                    slabfree = {"w1": P.snapw(), "w3": P.snapw()}
                    slabready = {}
                    slabstored = {"w1": None, "w3": None}
                    natfree = [P.snapw()]
                    bslfree = P.snapw()
                    bslready = {}
                    h1free = [P.snapw() for _ in range(3)]
                    h3free = [P.snapw() for _ in range(3)]
                    h1ready = [None] * 3
                    h3ready = [None] * 3
                    s1ready = [None] * 3
                    s1free = [P.snapw() for _ in range(3)]

                    def load_slab(mat, wi):
                        ow = OWS[wi]
                        base = D * OWB[wi]
                        srcf = w1t_s if mat == "w1" else w3t_s
                        buf = slw1 if mat == "w1" else slw3
                        sm = "s_sl1" if mat == "w1" else "s_sl3"
                        n = P.dma("sp", buf[:, 0:NDC * ow],
                                  srcf[base:base + D * ow].rearrange(
                                      "(p f) -> p f", p=128),
                                  waits=slabfree[mat], sem=sm)
                        slabready[(mat, wi)] = [(sm, n)]

                    natfree2 = [P.snapw(), P.snapw()]
                    wseq = [0]

                    def prep_window(mat, wi):
                        ow = OWS[wi]
                        nsub = ow // 128
                        wsrc = w1_v if mat == "w1" else w3_v
                        buf = slw1 if mat == "w1" else slw3
                        bufv = buf.rearrange("p (dc o) -> p dc o", o=ow)
                        smst = "s_psa" if mat == "w1" else "s_psb"
                        r0 = OWB[wi] // 128
                        nlds = {}

                        def issue_load(sub):
                            hi = (wseq[0] + sub) % 2
                            sm = "s_pna" if hi == 0 else "s_pnb"
                            n = P.dma("pool", natw[:, hi * D:(hi + 1) * D],
                                      wsrc[:, r0 + sub, :],
                                      waits=natfree2[hi], sem=sm)
                            nlds[sub] = (sm, n, hi)

                        issue_load(0)
                        if nsub > 1:
                            issue_load(1)
                        first_ev = [True, True]
                        gi = 0
                        for sub in range(nsub):
                            sm, nld, hi = nlds[sub]
                            for g0 in range(0, NDC, 8):
                                bi = gi % 2
                                bk = banks[bi]
                                lastpe = None
                                for j in range(8):
                                    dc = g0 + j
                                    w = []
                                    if j == 0:
                                        w = list(bankfree[bi])
                                        if g0 == 0:
                                            w.append((sm, nld))
                                    lastpe = transpose(
                                        P, bk[:, j * 128:(j + 1) * 128],
                                        natw[:, hi * D + dc * 128:
                                             hi * D + (dc + 1) * 128],
                                        ident[:, :],
                                        waits=w,
                                        inc="c_pe" if j == 7 else None)
                                wv_ = [("c_pe", lastpe)]
                                ei = gi % 2
                                if first_ev[ei]:
                                    wv_ += slabfree[mat]
                                    if slabstored[mat] is not None:
                                        wv_.append(slabstored[mat])
                                    first_ev[ei] = False
                                o_ap = bufv[:, g0:g0 + 8,
                                            sub * 128:(sub + 1) * 128]
                                i_ap = bk[:, 0:1024].rearrange(
                                    "p (g o) -> p g o", o=128)
                                if ei == 0:
                                    nev = cp_dve(P, o_ap, i_ap, waits=wv_)
                                    bankfree[bi] = [("c_dve", nev)]
                                else:
                                    nev = cp_act(P, o_ap, i_ap, waits=wv_)
                                    bankfree[bi] = [("c_act", nev)]
                                gi += 1
                            natfree2[(wseq[0] + sub) % 2] = \
                                [("c_pe", P.cnt["c_pe"])]
                            if sub + 2 < nsub:
                                issue_load(sub + 2)
                        wseq[0] += nsub
                        marks = [("c_dve", P.cnt["c_dve"]),
                                 ("c_act", P.cnt["c_act"])]
                        slabready[(mat, wi)] = marks
                        base = D * OWB[wi]
                        dstf = w1t_s if mat == "w1" else w3t_s
                        dview = dstf[base:base + D * ow].rearrange(
                            "(p f) -> p f", p=128)
                        nst = P.dma("sp", dview, buf[:, 0:NDC * ow],
                                    waits=marks, sem=smst)
                        slabstored[mat] = (smst, nst)

                    def load_bsl(wi):
                        ow = OWS[wi]
                        P.dma("sp", bsl[0:16, 0:ow],
                              b1t_s[0:R, OWB[wi]:OWB[wi] + ow],
                              waits=bslfree, sem="s_bsl")
                        n3 = P.dma("sp", bsl[0:16, 384:384 + ow],
                                   b3t_s[0:R, OWB[wi]:OWB[wi] + ow], sem="s_bsl")
                        bslready[wi] = ("s_bsl", n3)

                    if prep_a:
                        prep_window("w1", 0)
                        load_bsl(0)
                        prep_window("w3", 0)
                    else:
                        load_slab("w1", 0)
                        load_bsl(0)
                        load_slab("w3", 0)

                    for wi in range(len(OWS)):
                        ow = OWS[wi]
                        nob = ow // 128
                        oc0 = OWB[wi] // 128
                        for ob in range(nob):
                            w = list(h1free[ob])
                            if ob == 0:
                                w += slabready[("w1", wi)]
                                w += [bslready[wi], xr1_ready]
                            for dc in range(NDC):
                                mm(P, h1ps[ob][:, 0:512],
                                   slw1[:, dc * ow + ob * 128:
                                        dc * ow + (ob + 1) * 128],
                                   xnT[:, dc * TB:(dc + 1) * TB],
                                   start=(dc == 0), stop=False,
                                   waits=w if dc == 0 else ())
                            n = mm(P, h1ps[ob][:, 0:512],
                                   bsl[0:16, ob * 128:(ob + 1) * 128],
                                   xr1t[0:16, :],
                                   start=False, stop=True, inc="c_pe")
                            h1ready[ob] = ("c_pe", n)
                        slabfree["w1"] = [("c_pe", P.cnt["c_pe"])]
                        if wi + 1 < len(OWS):
                            if prep_a:
                                prep_window("w1", wi + 1)
                            else:
                                load_slab("w1", wi + 1)
                        for ob in range(nob):
                            n = P.eop("act",
                                      (lambda h, o=s1b[:, ob * 512:(ob + 1) * 512],
                                       i=h1ps[ob][:, 0:512]:
                                       h.activation(o, i, AF.Silu)),
                                      waits=[h1ready[ob]] + s1free[ob])
                            s1ready[ob] = ("c_act", n)
                            h1free[ob] = [("c_act", n)]
                        for ob in range(nob):
                            w = list(h3free[ob])
                            if ob == 0:
                                w += slabready[("w3", wi)]
                                w += [xr3_ready]
                            for dc in range(NDC):
                                mm(P, h3ps[ob][:, 0:512],
                                   slw3[:, dc * ow + ob * 128:
                                        dc * ow + (ob + 1) * 128],
                                   xnT[:, dc * TB:(dc + 1) * TB],
                                   start=(dc == 0), stop=False,
                                   waits=w if dc == 0 else ())
                            n = mm(P, h3ps[ob][:, 0:512],
                                   bsl[0:16, 384 + ob * 128:384 + (ob + 1) * 128],
                                   xr3t[0:16, :],
                                   start=False, stop=True, inc="c_pe")
                            h3ready[ob] = ("c_pe", n)
                        slabfree["w3"] = [("c_pe", P.cnt["c_pe"])]
                        bslfree = [("c_pe", P.cnt["c_pe"])]
                        if wi + 1 < len(OWS):
                            if prep_a:
                                prep_window("w3", wi + 1)
                            else:
                                load_slab("w3", wi + 1)
                            load_bsl(wi + 1)
                        for ob in range(nob):
                            n = P.eop("dve",
                                      (lambda h, o=gT_t[:, oc0 + ob, :],
                                       a=s1b[:, ob * 512:(ob + 1) * 512],
                                       i=h3ps[ob][:, 0:512]:
                                       h.tensor_tensor(o, a, i, OP.mult)),
                                      waits=[h3ready[ob], s1ready[ob]])
                            h3free[ob] = [("c_dve", n)]
                            s1free[ob] = [("c_dve", n)]
                    gT_ready = [("c_dve", P.cnt["c_dve"])]

                    for tc in range(TC):
                        for oc in range(NOC):
                            w = []
                            if tc == 0 and oc == 0:
                                w = list(gT_ready) + h1free[0]
                            mm(P, h1ps[0][:, tc * 16:(tc + 1) * 16],
                               gT_t[:, oc, tc * 128:(tc + 1) * 128],
                               a2t[:, oc * R:(oc + 1) * R],
                               start=(oc == 0), stop=(oc == NOC - 1),
                               waits=w, inc="c_pe" if oc == NOC - 1 else None)
                    nev = cp_dve(P, xrtm[:, 0:64], h1ps[0][:, 0:64],
                                 waits=[("c_pe", P.cnt["c_pe"])])
                    lastpe = None
                    for tc in range(TC):
                        w = []
                        if tc == 0:
                            w = [("c_dve", nev)] + bankfree[0]
                        lastpe = transpose(
                            P, banks[0][0:16, tc * 128:(tc + 1) * 128],
                            xrtm[:, tc * 16:(tc + 1) * 16], ident[:, :],
                            waits=w, inc="c_pe" if tc == TC - 1 else None)
                    nxr2 = cp_dve(P, xr2t[0:16, :], banks[0][0:16, 0:TB],
                                  waits=[("c_pe", lastpe)])
                    xr2_ready = ("c_dve", nxr2)

                with ExitStack() as es_b:
                    n_ops = TC if prep_b else 2 * TC
                    w2sl = [es_b.enter_context(
                        nc.sbuf_tensor(sfx + f"w2sl{i}", [128, 4 * 512], BF16))
                        for i in range(2)]
                    xres = [es_b.enter_context(
                        nc.sbuf_tensor(sfx + f"xres{i}", [128, TC * 512], F32))
                        for i in range(2)]
                    b2sl = es_b.enter_context(
                        nc.sbuf_tensor(sfx + "b2sl", [16, 512], BF16))
                    ops = [es_b.enter_context(
                        nc.psum_tensor(sfx + f"ops{i}", [128, 512], F32))
                        for i in range(n_ops)]
                    if prep_b:
                        natw2 = es_b.enter_context(
                            nc.sbuf_tensor(sfx + "natw2", [128, HID], BF16))
                        w2stg2 = es_b.enter_context(
                            nc.sbuf_tensor(sfx + "w2stg2", [128, HID], BF16))
                        trb = [es_b.enter_context(
                            nc.psum_tensor(sfx + f"trb{i}", [128, 1024], BF16))
                            for i in range(2)]
                        trbfree = [P.snapw(), P.snapw()]
                        natw2free = [P.snapw(), P.snapw()]
                        stg2free = [P.snapw()]
                        w2p_mark = {}
                    if prep_q:
                        natq = es_b.enter_context(
                            nc.sbuf_tensor(sfx + "natq", [128, 5120], BF16))
                        natqfree = [P.snapw()]

                    def prep_w2_chunk(dmc):
                        halves = [(0, 43), (43, 43)]
                        nldh = []
                        for hx, (o0, osz) in enumerate(halves):
                            hi = hx % 2
                            sm = "s_pna" if hi == 0 else "s_pnb"
                            n = P.dma("pool",
                                      natw2[:, hi * 5504:hi * 5504 + osz * 128],
                                      w2[dmc * 128:(dmc + 1) * 128,
                                         o0 * 128:(o0 + osz) * 128],
                                      waits=natw2free[hi], sem=sm)
                            nldh.append((sm, n))
                        first_ev = [True, True]
                        gidx = 0
                        for hx, (o0, osz) in enumerate(halves):
                            sm, nld = nldh[hx]
                            hi = hx % 2
                            for g0 in range(0, osz, 8):
                                gsz = min(8, osz - g0)
                                bi = gidx % 2
                                bk = trb[bi]
                                lastpe = None
                                for j in range(gsz):
                                    oc = o0 + g0 + j
                                    w = []
                                    if j == 0:
                                        w = list(trbfree[bi])
                                        if g0 == 0:
                                            w.append((sm, nld))
                                    lastpe = transpose(
                                        P, bk[:, j * 128:(j + 1) * 128],
                                        natw2[:, hi * 5504 + (g0 + j) * 128:
                                              hi * 5504 + (g0 + j + 1) * 128],
                                        ident[:, :],
                                        waits=w,
                                        inc="c_pe" if j == gsz - 1 else None)
                                ei = gidx % 2
                                wv_ = [("c_pe", lastpe)]
                                if first_ev[ei]:
                                    wv_ += stg2free[0]
                                    first_ev[ei] = False
                                oc_ap = w2stg2[:, (o0 + g0) * 128:
                                               (o0 + g0 + gsz) * 128]
                                if ei == 0:
                                    nev = cp_dve(P, oc_ap, bk[:, 0:gsz * 128],
                                                 waits=wv_)
                                    trbfree[bi] = [("c_dve", nev)]
                                else:
                                    nev = cp_act(P, oc_ap, bk[:, 0:gsz * 128],
                                                 waits=wv_)
                                    trbfree[bi] = [("c_act", nev)]
                                gidx += 1
                            natw2free[hi] = [("c_pe", P.cnt["c_pe"])]
                        smst = W2P_SEMS[dmc % 8]
                        nst = P.dma("sp", w2t_v2[:, :, dmc * 128:(dmc + 1) * 128],
                                    w2stg2[:, :].rearrange(
                                        "p (oc o) -> p oc o", o=128),
                                    waits=[("c_dve", P.cnt["c_dve"]),
                                           ("c_act", P.cnt["c_act"])], sem=smst)
                        w2p_mark[dmc] = (smst, nst)
                        stg2free[0] = [(smst, nst)]

                    def prep_wqe_chunk(dc):
                        nld = P.dma("pool", natq[:, 0:D],
                                    wq[dc * 128:(dc + 1) * 128, :],
                                    waits=natqfree[0], sem="s_pnb")
                        gv = natq[:, 0:D].rearrange("p (hk g dh) -> p hk g dh",
                                                    g=4, dh=128)
                        t1 = natq[:, D:D + KVD].rearrange(
                            "p (hk dh) -> p hk dh", dh=128)
                        t2 = natq[:, 0:KVD].rearrange(
                            "p (hk dh) -> p hk dh", dh=128)
                        out_v = natq[:, KVD:2 * KVD].rearrange(
                            "p (hk dh) -> p hk dh", dh=128)
                        P.eop("dve", (lambda h, o=t1, a=gv[:, :, 0, :],
                                      b_=gv[:, :, 1, :]: h.tensor_add(o, a, b_)),
                              waits=[("s_pnb", nld)])
                        P.eop("dve", (lambda h, o=t2, a=gv[:, :, 2, :],
                                      b_=gv[:, :, 3, :]: h.tensor_add(o, a, b_)))
                        nfold = P.eop("dve", (lambda h, o=out_v, a=t1, b_=t2:
                                              h.tensor_add(o, a, b_)))
                        nst = P.dma("sp", wqe_s[dc * 128:(dc + 1) * 128, :],
                                    natq[:, KVD:2 * KVD],
                                    waits=[("c_dve", nfold)], sem="s_psa")
                        natqfree[0] = [("s_psa", nst)]

                    ofree = [P.snapw() for _ in range(n_ops)]
                    xresfree = [P.snapw(), P.snapw()]
                    xres_ready = [None, None]
                    w2free = [P.snapw(), P.snapw()]
                    w2ready = {}
                    b2free = P.snapw()
                    b2ready = None

                    ocg_list = [(g0, min(4, NOC - g0)) for g0 in range(0, NOC, 4)]

                    def load_w2(dmw, gidx):
                        g0, gsz = ocg_list[gidx]
                        bi = gidx % 2
                        sm = "s_w2a" if bi == 0 else "s_w2b"
                        wl = list(w2free[bi])
                        if prep_b:
                            for r in range(4):
                                wl.append(w2p_mark[4 * dmw + r])
                        n = P.dma("sp",
                                  w2sl[bi][:, 0:gsz * 512].rearrange(
                                      "p (g o) -> p g o", o=512),
                                  w2t_v2[:, g0:g0 + gsz,
                                         dmw * 512:(dmw + 1) * 512],
                                  waits=wl, sem=sm)
                        w2ready[(dmw, gidx)] = (sm, n)

                    def load_xres(dmw):
                        bi = dmw % 2
                        sm = "s_xra" if bi == 0 else "s_xrb"
                        n = P.dma("sp",
                                  xres[bi][:, :].rearrange(
                                      "p (tt o) -> p tt o", o=512),
                                  src_v[:, tb * TC:(tb + 1) * TC,
                                        dmw * 512:(dmw + 1) * 512],
                                  waits=xresfree[bi], sem=sm)
                        xres_ready[bi] = (sm, n)

                    if prep_b:
                        for dmc in range(4):
                            prep_w2_chunk(dmc)
                    nb2 = P.dma("sp", b2sl[0:16, 0:512], b2t_s[0:R, 0:512],
                                waits=b2free, sem="s_b2")
                    b2ready = ("s_b2", nb2)
                    load_w2(0, 0)
                    load_xres(0)

                    for dmw in range(NDMW):
                        bi = dmw % 2
                        ob = 0 if prep_b else (dmw % 2) * TC
                        for gidx, (g0, gsz) in enumerate(ocg_list):
                            if gidx + 1 < len(ocg_list):
                                load_w2(dmw, gidx + 1)
                            elif dmw + 1 < NDMW:
                                load_w2(dmw + 1, 0)
                            if prep_b and dmw + 1 < NDMW and \
                                    gidx in (4, 9, 14, 19):
                                prep_w2_chunk(4 * (dmw + 1) + (4, 9, 14, 19).index(gidx))
                            if prep_q and gidx in (4, 9, 14, 19):
                                prep_wqe_chunk(4 * dmw + (4, 9, 14, 19).index(gidx))
                            wsl = w2sl[gidx % 2]
                            for j in range(gsz):
                                oc = g0 + j
                                for tc in range(TC):
                                    w = []
                                    if j == 0 and tc == 0:
                                        w.append(w2ready[(dmw, gidx)])
                                        if gidx == 0:
                                            w += gT_ready
                                            w.append(xr2_ready)
                                            w.append(b2ready)
                                    if gidx == 0 and j == 0:
                                        w = w + ofree[ob + tc]
                                    mm(P, ops[ob + tc][:, 0:512],
                                       gT_t[:, oc, tc * 128:(tc + 1) * 128],
                                       wsl[:, j * 512:(j + 1) * 512],
                                       start=(oc == 0), stop=False,
                                       waits=w,
                                       inc="c_pe" if (j == gsz - 1 and
                                                      tc == TC - 1) else None)
                            w2free[gidx % 2] = [("c_pe", P.cnt["c_pe"])]
                        oready = [None] * TC
                        for tc in range(TC):
                            n = mm(P, ops[ob + tc][:, 0:512],
                                   xr2t[0:16, tc * 128:(tc + 1) * 128],
                                   b2sl[0:16, 0:512],
                                   start=False, stop=True, inc="c_pe")
                            oready[tc] = ("c_pe", n)
                        for tc in range(TC):
                            n = P.eop("dve",
                                      (lambda h,
                                       o=xres[bi][:, tc * 512:(tc + 1) * 512],
                                       i=ops[ob + tc][:, 0:512],
                                       x_=xres[bi][:, tc * 512:(tc + 1) * 512]:
                                       h.scalar_tensor_tensor(
                                           o, i, 0.5, x_, OP.mult, OP.add)),
                                      waits=[oready[tc], xres_ready[bi]])
                            ofree[ob + tc] = [("c_dve", n)]
                        smst = "s_sta" if bi == 0 else "s_stb"
                        nst = P.dma("sp",
                                    dst_v[:, tb * TC:(tb + 1) * TC,
                                          dmw * 512:(dmw + 1) * 512],
                                    xres[bi][:, :].rearrange(
                                        "p (tt o) -> p tt o", o=512),
                                    waits=[("c_dve", P.cnt["c_dve"])], sem=smst)
                        xresfree[bi] = [(smst, nst)]
                        if dmw + 1 < NDMW:
                            load_xres(dmw + 1)
                            nb2 = P.dma("sp", b2sl[0:16, 0:512],
                                        b2t_s[0:R, (dmw + 1) * 512:
                                              (dmw + 2) * 512],
                                        waits=[("c_pe", P.cnt["c_pe"])],
                                        sem="s_b2")
                            b2ready = ("s_b2", nb2)
                    if prep_q:
                        pass
            P.barrier()
            P.play()

    ffn(x_in, x1h, "f1", fuse_prep=True)

    with ExitStack() as es2:
        Qfm = es2.enter_context(nc.sbuf_tensor("Qfm", [128, NH * TOK], BF16))
        VALfm = es2.enter_context(nc.sbuf_tensor("VALfm", [128, NH * TOK], BF16))

        with ExitStack() as es_a:
            xnT = es_a.enter_context(nc.sbuf_tensor("a_xnT", [128, NDC * TB], BF16))
            xbuf = es_a.enter_context(nc.sbuf_tensor("a_xbuf", [128, D], F32))
            xnb = es_a.enter_context(nc.sbuf_tensor("a_xnb", [128, D], BF16))
            stats = es_a.enter_context(nc.sbuf_tensor("a_stats", [128, 8], F32))
            wqsl = [es_a.enter_context(
                nc.sbuf_tensor(f"a_wqsl{i}", [128, NDC * 128], BF16))
                for i in range(2)]
            kvw_b = es_a.enter_context(
                nc.sbuf_tensor("a_kvw", [128, NDC * 512], BF16))
            kvout = es_a.enter_context(
                nc.sbuf_tensor("a_kvout", [128, TC * 512], BF16))
            kqps = [es_a.enter_context(
                nc.psum_tensor(f"a_kq{i}", [128, 512], F32)) for i in range(TC)]
            trp_ = es_a.enter_context(nc.psum_tensor("a_trp", [128, 1024], BF16))
            trp2_ = es_a.enter_context(nc.psum_tensor("a_trp2", [128, 1024], BF16))

            banks = [trp_, trp2_]
            bankfree = [P.snapw(), P.snapw()]
            xnT_t = xnT.rearrange("p (dc t) -> p dc t", t=TB)

            wqe_v = wqe_s.rearrange("(dc p) f -> p dc f", p=128)
            wk_v = wk.rearrange("(dc p) f -> p dc f", p=128)
            wv_v = wv.rearrange("(dc p) f -> p dc f", p=128)
            kxo_v = kx_own.rearrange("(tt p) f -> p tt f", p=128)
            x1_v = x1h.rearrange("(tt p) d -> p tt d", p=128)

            wq_free = [P.snapw(), P.snapw()]
            kv_free = P.snapw()
            kvo_free = P.snapw()
            pfree = [P.snapw() for _ in range(TC)]
            xbuf_free = P.snapw()
            xnb_free = P.snapw()
            stats_free = P.snapw()

            for tb in range(NTB):
                t0 = tb * TB
                for tc in range(TC):
                    nx, nsq, nxn = emit_ln_tc(
                        P, x1_v[:, tb * TC + tc, :], xbuf, xnb, stats,
                        EPS_AP, xbuf_free, xnb_free, stats_free)
                    lastpe = emit_transpose_fm(P, ident, banks, bankfree,
                                               xnb, nxn, xnT_t, tc)
                    xbuf_free = [("c_dve", nxn), ("c_act", nsq)]
                    xnb_free = [("c_pe", lastpe)]
                    stats_free = [("c_dve", nxn)]
                xnT_ready = ("c_dve", P.cnt["c_dve"])

                for fc in range(NH):
                    bi = fc % 2
                    smq = "s_wqa" if bi == 0 else "s_wqb"
                    nwq = P.dma("sp",
                                wqsl[bi][:, :].rearrange(
                                    "p (dc f) -> p dc f", f=128),
                                wqe_v[:, :, fc * 128:(fc + 1) * 128],
                                waits=wq_free[bi], sem=smq)
                    tcq = fc % TC
                    lastpe = None
                    for dc in range(NDC):
                        w = []
                        if dc == 0:
                            w = [(smq, nwq), xnT_ready] + pfree[tcq]
                        lastpe = mm(P, kqps[tcq][:, 0:512],
                                    wqsl[bi][:, dc * 128:(dc + 1) * 128],
                                    xnT_t[:, dc, :],
                                    start=(dc == 0), stop=(dc == NDC - 1),
                                    waits=w,
                                    inc="c_pe" if dc == NDC - 1 else None)
                    wq_free[bi] = [("c_pe", lastpe)]
                    nev = cp_dve(P, Qfm[:, fc * TOK + t0:fc * TOK + t0 + TB],
                                 kqps[tcq][:, 0:512], waits=[("c_pe", lastpe)])
                    pfree[tcq] = [("c_dve", nev)]

                for (wsrc, colbase) in ((wk_v, 0), (wv_v, KVD)):
                    for kvw in range(2):
                        nld = P.dma("pool",
                                    kvw_b[:, :].rearrange(
                                        "p (dc f) -> p dc f", f=512),
                                    wsrc[:, :, kvw * 512:(kvw + 1) * 512],
                                    waits=kv_free, sem="s_kv")
                        oready = [None] * TC
                        for tc in range(TC):
                            lastpe = None
                            for dc in range(NDC):
                                w = []
                                if dc == 0:
                                    w = [("s_kv", nld), xnT_ready] + pfree[tc]
                                lastpe = mm(
                                    P, kqps[tc][:, 0:512],
                                    xnT_t[:, dc, tc * 128:(tc + 1) * 128],
                                    kvw_b[:, dc * 512:(dc + 1) * 512],
                                    start=(dc == 0), stop=(dc == NDC - 1),
                                    waits=w,
                                    inc="c_pe" if dc == NDC - 1 else None)
                            oready[tc] = ("c_pe", lastpe)
                        kv_free = [("c_pe", P.cnt["c_pe"])]
                        for tc in range(TC):
                            w = [oready[tc]]
                            if tc == 0:
                                w += kvo_free
                            nev = cp_dve(P, kvout[:, tc * 512:(tc + 1) * 512],
                                         kqps[tc][:, 0:512], waits=w)
                            pfree[tc] = [("c_dve", nev)]
                        nst = P.dma("sp",
                                    kxo_v[:, tb * TC:(tb + 1) * TC,
                                          colbase + kvw * 512:
                                          colbase + (kvw + 1) * 512],
                                    kvout[:, :].rearrange(
                                        "p (tt o) -> p tt o", o=512),
                                    waits=[("c_dve", P.cnt["c_dve"])],
                                    sem="s_kst")
                        kvo_free = [("s_kst", nst)]

            P.barrier()
            P.play()

        P.wait("pool", "s_kst", P.cnt["s_kst"])

        def cc_fn(h):
            return h.collective_compute(
                "AllGather", OP.bypass,
                replica_groups=[[0, 1], [2, 3], [4, 5], [6, 7]],
                ins=[kx_own], outs=[kx_all])
        P.op("pool", cc_fn, inc="cc")
        P.play()

        with ExitStack() as es_d:
            Kfm = es_d.enter_context(nc.sbuf_tensor("Kfm", [128, NH * S], BF16))
            ktmp = es_d.enter_context(nc.sbuf_tensor("d_ktmp", [128, KVD], BF16))
            Vh = es_d.enter_context(nc.sbuf_tensor("d_Vh", [128, S], BF16))
            Pb = es_d.enter_context(nc.sbuf_tensor("d_P", [128, S], BF16))
            PT = es_d.enter_context(nc.sbuf_tensor("d_PT", [128, NKC * 512], BF16))
            sums = es_d.enter_context(nc.sbuf_tensor("d_sums", [128, NKB + 2], F32))
            scps = [es_d.enter_context(
                nc.psum_tensor(f"d_sc{i}", [128, 512], F32)) for i in range(4)]
            pvps = es_d.enter_context(nc.psum_tensor("d_pv", [128, 512], F32))
            trp_ = es_d.enter_context(nc.psum_tensor("d_trp", [128, 1024], BF16))
            trp2_ = es_d.enter_context(nc.psum_tensor("d_trp2", [128, 1024], BF16))

            banks = [trp_, trp2_]
            bankfree = [P.snapw(), P.snapw()]

            P.wait("sp", "cc", P.cnt["cc"])
            Kfm_t = Kfm.rearrange("p (hh k) -> p hh k", k=S)
            ktfree = P.snapw()
            for kc in range(NKC):
                nld = P.dma("sp", ktmp[:, :],
                            kx_all[kc * 128:(kc + 1) * 128, 0:KVD],
                            waits=ktfree, sem="s_kt")
                bi = kc % 2
                bk = banks[bi]
                lastpe = None
                for hh in range(NH):
                    w = []
                    if hh == 0:
                        w = [("s_kt", nld)] + bankfree[bi]
                    lastpe = transpose(P, bk[:, hh * 128:(hh + 1) * 128],
                                       ktmp[:, hh * 128:(hh + 1) * 128],
                                       ident[:, :],
                                       waits=w, inc="c_pe" if hh == NH - 1 else None)
                ktfree = [("c_pe", lastpe)]
                nev = cp_dve(P, Kfm_t[:, :, kc * 128:(kc + 1) * 128],
                             bk[:, :].rearrange("p (j o) -> p j o", o=128),
                             waits=[("c_pe", lastpe)])
                bankfree[bi] = [("c_dve", nev)]
            kfm_ready = ("c_dve", P.cnt["c_dve"])

            kx_v = kx_all.rearrange("(rk p) f -> p rk f", p=128)
            PT_t = PT.rearrange("p (kc q) -> p kc q", q=512)

            scfree = [P.snapw() for _ in range(4)]
            pvfree = P.snapw()
            vh_free = P.snapw()
            sums_free = P.snapw()
            first_sc = [kfm_ready]
            for hh in range(NH):
                nv = P.dma("sp", Vh[:, :].rearrange("p (kc o) -> p kc o", o=128),
                           kx_v[:, 0:NKC, KVD + hh * 128:KVD + (hh + 1) * 128],
                           waits=vh_free, sem="s_vh")
                vh_ready = ("s_vh", nv)
                for qg in range(TOK // 512):
                    for qt in range(TC):
                        q0 = qg * 512 + qt * 128
                        first_exp = True
                        for kb in range(NKB):
                            if True:
                                j = kb % 4
                                w = list(first_sc) + scfree[j]
                                first_sc = []
                                n = mm(P, scps[j][:, 0:512],
                                       Qfm[:, hh * TOK + q0:hh * TOK + q0 + 128],
                                       Kfm_t[:, hh, kb * 512:(kb + 1) * 512],
                                       start=True, stop=True,
                                       waits=w, inc="c_pe")
                                wexp = [("c_pe", n)]
                                if first_exp:
                                    wexp += sums_free
                                    first_exp = False
                                nex = P.eop("act",
                                            (lambda h,
                                             o=Pb[:, kb * 512:(kb + 1) * 512],
                                             i=scps[j][:, 0:512],
                                             b_=SMB_AP, s_=SMS_AP,
                                             a=sums[:, kb:kb + 1]:
                                             h.activation(o, i, AF.Exp, bias=b_,
                                                          scale=s_, accum_out=a)),
                                            waits=wexp)
                                scfree[j] = [("c_act", nex)]
                        P.eop("dve", (lambda h, o=sums[:, NKB:NKB + 1],
                                      i=sums[:, 0:NKB]:
                                      h.reduce_sum(o, i, axis=AX)),
                              waits=[("c_act", P.cnt["c_act"])])
                        nrec = P.eop("dve", (lambda h, o=sums[:, NKB + 1:NKB + 2],
                                             i=sums[:, NKB:NKB + 1]:
                                             h.reciprocal(o, i)))
                        sums_free = [("c_dve", nrec)]
                        nnorm = P.eop("dve", (lambda h, o=Pb[:, :], i=Pb[:, :],
                                              s_=sums[:, NKB + 1:NKB + 2]:
                                              h.tensor_scalar_mul(o, i, s_)))
                        for gidx, g0 in enumerate(range(0, NKC, 8)):
                            bi = gidx % 2
                            bk = banks[bi]
                            lastpe = None
                            for j in range(8):
                                kc = g0 + j
                                w = []
                                if j == 0:
                                    w = [("c_dve", nnorm)] + bankfree[bi]
                                lastpe = transpose(
                                    P, bk[:, j * 128:(j + 1) * 128],
                                    Pb[:, kc * 128:(kc + 1) * 128], ident[:, :],
                                    waits=w, inc="c_pe" if j == 7 else None)
                            nev = cp_dve(P, PT_t[:, g0:g0 + 8,
                                                 qt * 128:(qt + 1) * 128],
                                         bk[:, :].rearrange(
                                             "p (j o) -> p j o", o=128),
                                         waits=[("c_pe", lastpe)])
                            bankfree[bi] = [("c_dve", nev)]
                    pt_ready = ("c_dve", P.cnt["c_dve"])
                    lastpe = None
                    for kc in range(NKC):
                        w = []
                        if kc == 0:
                            w = [pt_ready, vh_ready] + pvfree
                        lastpe = mm(P, pvps[:, 0:512],
                                    Vh[:, kc * 128:(kc + 1) * 128],
                                    PT_t[:, kc, :],
                                    start=(kc == 0), stop=(kc == NKC - 1),
                                    waits=w,
                                    inc="c_pe" if kc == NKC - 1 else None)
                    nev = cp_dve(P, VALfm[:, hh * TOK + qg * 512:
                                          hh * TOK + (qg + 1) * 512],
                                 pvps[:, 0:512], waits=[("c_pe", lastpe)])
                    pvfree = [("c_dve", nev)]
                vh_free = [("c_pe", P.cnt["c_pe"])]
            val_ready = ("c_dve", P.cnt["c_dve"])
            P.barrier()
            P.play()

        with ExitStack() as es_e:
            wob = es_e.enter_context(nc.sbuf_tensor("e_wob", [128, NH * 512], BF16))
            xres = [es_e.enter_context(
                nc.sbuf_tensor(f"e_xres{i}", [128, TC * 512], F32))
                for i in range(2)]
            ops = [es_e.enter_context(
                nc.psum_tensor(f"e_ops{i}", [128, 512], F32)) for i in range(4)]

            wo_v = wo.rearrange("(kc p) d -> p kc d", p=128)
            x1_v2 = x1h.rearrange("(tt p) d -> p tt d", p=128)
            x2_v = x2h.rearrange("(tt p) d -> p tt d", p=128)

            wofree = P.snapw()
            ofree = [P.snapw() for _ in range(4)]
            xresfree = [P.snapw(), P.snapw()]

            for tg in range(TOK // 512):
                for dmw in range(NDMW):
                    nwo = P.dma("pool",
                                wob[:, :].rearrange("p (kc o) -> p kc o", o=512),
                                wo_v[:, :, dmw * 512:(dmw + 1) * 512],
                                waits=wofree, sem="s_wo")
                    woready = ("s_wo", nwo)
                    bi = dmw % 2
                    smx = "s_xra" if bi == 0 else "s_xrb"
                    nxr = P.dma("sp",
                                xres[bi][:, :].rearrange(
                                    "p (tt o) -> p tt o", o=512),
                                x1_v2[:, tg * TC:(tg + 1) * TC,
                                      dmw * 512:(dmw + 1) * 512],
                                waits=xresfree[bi], sem=smx)
                    oready = [None] * 4
                    for tt in range(4):
                        lastpe = None
                        for kvc in range(NH):
                            w = []
                            if kvc == 0:
                                w = [woready, val_ready] + ofree[tt]
                            lastpe = mm(
                                P, ops[tt][:, 0:512],
                                VALfm[:, kvc * TOK + tg * 512 + tt * 128:
                                      kvc * TOK + tg * 512 + (tt + 1) * 128],
                                wob[:, kvc * 512:(kvc + 1) * 512],
                                start=(kvc == 0), stop=(kvc == NH - 1),
                                waits=w, inc="c_pe" if kvc == NH - 1 else None)
                        oready[tt] = ("c_pe", lastpe)
                    wofree = [("c_pe", P.cnt["c_pe"])]
                    for tt in range(4):
                        n = P.eop("dve",
                                  (lambda h,
                                   o=xres[bi][:, tt * 512:(tt + 1) * 512],
                                   i=ops[tt][:, 0:512],
                                   x_=xres[bi][:, tt * 512:(tt + 1) * 512]:
                                   h.tensor_tensor(o, i, x_, OP.add)),
                                  waits=[oready[tt], (smx, nxr)])
                        ofree[tt] = [("c_dve", n)]
                    smst = "s_sta" if bi == 0 else "s_stb"
                    nst = P.dma("sp",
                                x2_v[:, tg * TC:(tg + 1) * TC,
                                     dmw * 512:(dmw + 1) * 512],
                                xres[bi][:, :].rearrange(
                                    "p (tt o) -> p tt o", o=512),
                                waits=[("c_dve", P.cnt["c_dve"])], sem=smst)
                    xresfree[bi] = [(smst, nst)]
            P.barrier()
            P.play()

    ffn(x2h, y_out, "f3")

    for smn in SNAP_SEMS:
        P.wait("sp", smn, P.cnt[smn])
    P.play()

    es_top.close()
    return nc



def kernel(L0_attn_x, L0_attn_wq, L0_attn_wk, L0_attn_wv, L0_attn_wo,
           L0_ffn_w1, L0_ffn_w3, L0_ffn_w2,
           L0_ffn_lora_A1, L0_ffn_lora_B1, L0_ffn_lora_A3, L0_ffn_lora_B3,
           L0_ffn_lora_A2, L0_ffn_lora_B2):
    x = np.ascontiguousarray(np.asarray(L0_attn_x, dtype=np.float32))
    x2d = x.reshape(B * S, D)
    ident = np.eye(128, dtype=ml_dtypes.bfloat16)
    consts = np.tile(np.array([[EPS, SM_BIAS, SM_SCALE, 0.0]], dtype=np.float32),
                     (128, 1))
    common = {
        "wq": np.ascontiguousarray(np.asarray(L0_attn_wq, dtype=np.float32)),
        "wk": np.ascontiguousarray(np.asarray(L0_attn_wk, dtype=np.float32)),
        "wv": np.ascontiguousarray(np.asarray(L0_attn_wv, dtype=np.float32)),
        "wo": np.ascontiguousarray(np.asarray(L0_attn_wo, dtype=np.float32)),
        "w1": np.ascontiguousarray(np.asarray(L0_ffn_w1, dtype=np.float32)),
        "w3": np.ascontiguousarray(np.asarray(L0_ffn_w3, dtype=np.float32)),
        "w2": np.ascontiguousarray(np.asarray(L0_ffn_w2, dtype=np.float32)),
        "a1": np.ascontiguousarray(np.asarray(L0_ffn_lora_A1, dtype=np.float32)),
        "b1": np.ascontiguousarray(np.asarray(L0_ffn_lora_B1, dtype=np.float32)),
        "a3": np.ascontiguousarray(np.asarray(L0_ffn_lora_A3, dtype=np.float32)),
        "b3": np.ascontiguousarray(np.asarray(L0_ffn_lora_B3, dtype=np.float32)),
        "a2": np.ascontiguousarray(np.asarray(L0_ffn_lora_A2, dtype=np.float32)),
        "b2": np.ascontiguousarray(np.asarray(L0_ffn_lora_B2, dtype=np.float32)),
        "identw": ident,
        "consts": consts,
    }
    in_maps = []
    for c in range(NCORE):
        m = dict(common)
        m["x_in"] = np.ascontiguousarray(x2d[c * TOK:(c + 1) * TOK, :])
        in_maps.append(m)

    nc = build_program()
    res = run_bass_kernel_spmd(nc, in_maps, core_ids=list(range(NCORE)))
    y = np.concatenate([res.results[c]["y_out"] for c in range(NCORE)], axis=0)
    return y.reshape(B, S, D)


if __name__ == "__main__":
    import time
    t0 = time.time()
    nc = build_program()
    print(f"build ok in {time.time()-t0:.1f}s")


# version 06
# speedup vs baseline: 1.0223x; 1.0223x over previous
import numpy as np
import ml_dtypes
from contextlib import ExitStack

import concourse.bass as bass
import concourse.mybir as mybir
from concourse.bass_utils import run_bass_kernel_spmd

F32 = mybir.dt.float32
BF16 = mybir.dt.bfloat16
AF = mybir.ActivationFunctionType
OP = mybir.AluOpType
AX = mybir.AxisListType.X

D = 4096
HID = 11008
KVD = 1024
NH = 8
R = 16
B = 4
S = 4096
NCORE = 8
TOK = 2048
TB = 512
NTB = TOK // TB
TC = TB // 128
NDC = D // 128
NOC = HID // 128
OWS = [384] * 28 + [256]
OWB = [0]
for _w in OWS:
    OWB.append(OWB[-1] + _w)
DMW = 512
NDMW = D // DMW
NKC = S // 128
NKB = S // 512
EPS = 1e-5
SM_SCALE = 1.0 / float(np.sqrt(128.0))
SM_BIAS = -12.0

ENGS = ("pe", "act", "dve", "pool", "sp")
SEM_NAMES = ["c_pe", "c_act", "c_dve", "c_pool", "q_sp", "q_pool", "cc",
             "s_ln", "s_sl1", "s_sl3", "s_bsl", "s_w2a", "s_w2b",
             "s_xra", "s_xrb", "s_sta", "s_stb", "s_b2",
             "s_wqa", "s_wqb", "s_kv", "s_kst", "s_kt", "s_vh", "s_wo",
             "s_pna", "s_pnb", "s_psa", "s_psb"] + [f"s_w2p{i}" for i in range(8)]
SNAP_SEMS = [n for n in SEM_NAMES if n != "cc"]

def configure(tok, seq):
    global TOK, S, NTB, NKC, NKB
    TOK = tok
    S = seq
    NTB = TOK // TB
    NKC = S // 128
    NKB = S // 512


class Plan:

    def __init__(self, nc, semh):
        self.nc = nc
        self.semh = semh
        self.items = {e: [] for e in ENGS}
        self.cnt = {name: 0 for name in semh}

    def snapw(self):
        return [(s, self.cnt[s]) for s in SNAP_SEMS if self.cnt[s] > 0]

    def wait(self, eng, sem, val):
        if val <= 0:
            return
        assert val <= self.cnt[sem], \
            f"wait on future count: {sem} {val} > {self.cnt[sem]}"
        self.items[eng].append(("w", sem, val))

    def waitl(self, eng, waits):
        for (s, v) in waits:
            self.wait(eng, s, v)

    def op(self, eng, fn, inc=None, inc_k=1):
        n = None
        if inc is not None:
            self.cnt[inc] += inc_k
            n = self.cnt[inc]
        self.items[eng].append(("o", fn, inc, inc_k))
        return n

    def eop(self, eng, fn, waits=()):
        ch = "c_" + eng
        self.waitl(eng, waits)
        self.wait(eng, ch, self.cnt[ch])
        return self.op(eng, fn, inc=ch)

    def dma(self, q, out, in_, waits=(), sem=None):
        if sem is None:
            sem = "q_sp" if q == "sp" else "q_pool"
        self.waitl(q, waits)

        def fn(h, out=out, in_=in_):
            return h.dma_start(out=out, in_=in_)
        return self.op(q, fn, inc=sem, inc_k=16)

    def barrier(self):
        snap = {s: self.cnt[s] for s in self.cnt}
        for e in ENGS:
            for s, v in snap.items():
                if v > 0:
                    self.items[e].append(("w", s, v))

    def play(self):
        nc = self.nc

        def runner(eng):
            items = self.items[eng]
            semh = self.semh

            def go(h):
                for it in items:
                    if it[0] == "w":
                        h.wait_ge(semh[it[1]], it[2])
                    else:
                        _, fn, inc, k = it
                        ins = fn(h)
                        if inc is not None:
                            ins.then_inc(semh[inc], k)
            return go

        with nc.Block() as block:
            block.tensor(runner("pe"))
            block.scalar(runner("act"))
            block.vector(runner("dve"))
            block.gpsimd(runner("pool"))
            block.sync(runner("sp"))
        self.items = {e: [] for e in ENGS}


def mm(P, out_ap, lhsT_ap, rhs_ap, start, stop, waits=(), inc=None):
    P.waitl("pe", waits)

    def fn(h, o=out_ap, l=lhsT_ap, r=rhs_ap, st=start, sp=stop):
        return h.matmul(o, l, r, start=st, stop=sp)
    return P.op("pe", fn, inc=inc)


def transpose(P, out_ap, in_ap, ident_ap, waits=(), inc=None):
    P.waitl("pe", waits)

    def fn(h, o=out_ap, i=in_ap, d=ident_ap):
        return h.transpose(o, i, d)
    return P.op("pe", fn, inc=inc)


def cp_dve(P, out_ap, in_ap, waits=()):
    return P.eop("dve", (lambda h, o=out_ap, i=in_ap: h.tensor_copy(o, i)),
                 waits=waits)


def cp_act(P, out_ap, in_ap, waits=()):
    return P.eop("act", (lambda h, o=out_ap, i=in_ap: h.copy(o, i)),
                 waits=waits)


def emit_ln_tc(P, src_row_ap, xbuf, xnb, stats, eps_ap, xbuf_free, xnb_free,
               stats_free):
    nx = P.dma("sp", xbuf[:, :], src_row_ap, waits=xbuf_free, sem="s_ln")
    P.eop("dve", (lambda h, o=stats[:, 0:1], i=xbuf[:, :]:
                  h.reduce_sum(o, i, axis=AX)),
          waits=[("s_ln", nx)] + stats_free)
    nsq = P.eop("act", (lambda h, o=xnb[:, :], i=xbuf[:, :], a=stats[:, 3:4]:
                        h.activation(o, i, AF.Square, accum_out=a)),
                waits=[("s_ln", nx)] + xnb_free + stats_free)
    P.eop("dve", (lambda h, o=stats[:, 1:2], i=stats[:, 0:1]:
                  h.tensor_scalar_mul(o, i, -1.0 / D)))
    P.eop("dve", (lambda h, o=stats[:, 2:3], i=stats[:, 1:2]:
                  h.tensor_mul(o, i, i)))
    nvar = P.eop("dve", (lambda h, o=stats[:, 4:5], i=stats[:, 3:4],
                         m=stats[:, 2:3]:
                         h.scalar_tensor_tensor(o, i, 1.0 / D, m,
                                                OP.mult, OP.subtract)),
                 waits=[("c_act", nsq)])
    nstd = P.eop("act", (lambda h, o=stats[:, 5:6], i=stats[:, 4:5], b_=eps_ap:
                         h.activation(o, i, AF.Sqrt, bias=b_)),
                 waits=[("c_dve", nvar)])
    P.eop("dve", (lambda h, o=stats[:, 6:7], i=stats[:, 5:6]:
                  h.reciprocal(o, i)),
          waits=[("c_act", nstd)])
    nxn = P.eop("dve", (lambda h, o=xnb[:, :], i=xbuf[:, :],
                        s1=stats[:, 1:2], s2=stats[:, 6:7]:
                        h.tensor_scalar(o, i, s1, s2, OP.add, OP.mult)),
                waits=[("c_act", nsq)])
    return nx, nsq, nxn


def emit_transpose_fm(P, ident, banks, bankfree, xnb, nxn, xnT_t, tc):
    lastpe = None
    for gidx, g0 in enumerate(range(0, NDC, 8)):
        bi = gidx % 2
        bk = banks[bi]
        for j in range(8):
            dc = g0 + j
            w = []
            if j == 0:
                w = [("c_dve", nxn)] + bankfree[bi]
            lastpe = transpose(P, bk[:, j * 128:(j + 1) * 128],
                               xnb[:, dc * 128:(dc + 1) * 128], ident[:, :],
                               waits=w, inc="c_pe" if j == 7 else None)
        nev = cp_dve(P, xnT_t[:, g0:g0 + 8, tc * 128:(tc + 1) * 128],
                     bk[:, :].rearrange("p (j o) -> p j o", o=128),
                     waits=[("c_pe", lastpe)])
        bankfree[bi] = [("c_dve", nev)]
    return lastpe


def build_program():
    nc = bass.Bass("TRN2", target_bir_lowering=False)

    x_in = nc.dram_tensor("x_in", [TOK, D], F32, kind="ExternalInput").ap()
    wq = nc.dram_tensor("wq", [D, D], F32, kind="ExternalInput").ap()
    wk = nc.dram_tensor("wk", [D, KVD], F32, kind="ExternalInput").ap()
    wv = nc.dram_tensor("wv", [D, KVD], F32, kind="ExternalInput").ap()
    wo = nc.dram_tensor("wo", [KVD, D], F32, kind="ExternalInput").ap()
    w1 = nc.dram_tensor("w1", [HID, D], F32, kind="ExternalInput").ap()
    w3 = nc.dram_tensor("w3", [HID, D], F32, kind="ExternalInput").ap()
    w2 = nc.dram_tensor("w2", [D, HID], F32, kind="ExternalInput").ap()
    a1 = nc.dram_tensor("a1", [R, D], F32, kind="ExternalInput").ap()
    b1 = nc.dram_tensor("b1", [HID, R], F32, kind="ExternalInput").ap()
    a3 = nc.dram_tensor("a3", [R, D], F32, kind="ExternalInput").ap()
    b3 = nc.dram_tensor("b3", [HID, R], F32, kind="ExternalInput").ap()
    a2 = nc.dram_tensor("a2", [R, HID], F32, kind="ExternalInput").ap()
    b2 = nc.dram_tensor("b2", [D, R], F32, kind="ExternalInput").ap()
    identw = nc.dram_tensor("identw", [128, 128], BF16, kind="ExternalInput").ap()
    consts = nc.dram_tensor("consts", [128, 4], F32, kind="ExternalInput").ap()
    y_out = nc.dram_tensor("y_out", [TOK, D], F32, kind="ExternalOutput").ap()

    x1h = nc.dram_tensor("x1h", [TOK, D], F32, kind="Internal").ap()
    x2h = nc.dram_tensor("x2h", [TOK, D], F32, kind="Internal").ap()
    w1t_s = nc.dram_tensor("w1t_s", [D * HID], BF16, kind="Internal").ap()
    w3t_s = nc.dram_tensor("w3t_s", [D * HID], BF16, kind="Internal").ap()
    w2t_s = nc.dram_tensor("w2t_s", [HID, D], BF16, kind="Internal").ap()
    wqe_s = nc.dram_tensor("wqe_s", [D, KVD], BF16, kind="Internal").ap()
    b1t_s = nc.dram_tensor("b1t_s", [R, HID], BF16, kind="Internal").ap()
    b3t_s = nc.dram_tensor("b3t_s", [R, HID], BF16, kind="Internal").ap()
    b2t_s = nc.dram_tensor("b2t_s", [R, D], BF16, kind="Internal").ap()
    kx_own = nc.dram_tensor("kx_own", [TOK, 2 * KVD], BF16, kind="Internal").ap()
    kx_all = nc.dram_tensor("kx_all", [S, 2 * KVD], BF16, kind="Internal").ap()

    es_top = ExitStack()
    semh = {name: es_top.enter_context(nc.semaphore(name)) for name in SEM_NAMES}

    ident = es_top.enter_context(nc.sbuf_tensor("ident", [128, 128], BF16))
    csb = es_top.enter_context(nc.sbuf_tensor("csb", [128, 4], F32))
    a1t = es_top.enter_context(nc.sbuf_tensor("a1t", [128, NDC * R], BF16))
    a3t = es_top.enter_context(nc.sbuf_tensor("a3t", [128, NDC * R], BF16))
    a2t = es_top.enter_context(nc.sbuf_tensor("a2t", [128, NOC * R], BF16))

    P = Plan(nc, semh)
    EPS_AP = csb[:, 0:1]
    SMB_AP = csb[:, 1:2]
    SMS_AP = csb[:, 2:3]

    with ExitStack() as es:
        natb = es.enter_context(nc.sbuf_tensor("p0nat", [128, NOC * R], BF16))
        stage = es.enter_context(nc.sbuf_tensor("p0stg", [16, NOC * 128], BF16))
        anat = es.enter_context(nc.sbuf_tensor("p0anat", [16, NOC * 128], BF16))
        trp_ = es.enter_context(nc.psum_tensor("p0trp", [128, 1024], BF16))
        trp2_ = es.enter_context(nc.psum_tensor("p0trp2", [128, 1024], BF16))
        banks = [trp_, trp2_]
        bankfree = [[], []]

        P.dma("sp", ident[:, :], identw)
        P.dma("sp", csb[:, :], consts)
        P.wait("pe", "q_sp", P.cnt["q_sp"])

        def prep_at(a_src, dst, nchunk):
            nld = P.dma("pool", anat[0:16, 0:nchunk * 128], a_src,
                        waits=[("c_pe", P.cnt["c_pe"])])
            for gidx, g0 in enumerate(range(0, nchunk, 8)):
                gsz = min(8, nchunk - g0)
                bi = gidx % 2
                bk = banks[bi]
                lastpe = None
                for j in range(gsz):
                    c = g0 + j
                    w = []
                    if j == 0:
                        w = [("q_pool", nld)] + bankfree[bi]
                    lastpe = transpose(P, bk[:, j * 16:(j + 1) * 16],
                                       anat[0:16, c * 128:(c + 1) * 128],
                                       ident[0:16, 0:16],
                                       waits=w, inc="c_pe" if j == gsz - 1 else None)
                nev = cp_dve(P, dst[:, g0 * R:(g0 + gsz) * R],
                             bk[:, 0:gsz * R], waits=[("c_pe", lastpe)])
                bankfree[bi] = [("c_dve", nev)]

        prep_at(a1, a1t, NDC)
        prep_at(a3, a3t, NDC)
        prep_at(a2, a2t, NOC)

        def prep_bt(b_src, dst_dram, nchunk):
            src_v = b_src.rearrange("(c p) r -> p c r", p=128)
            nld = P.dma("pool",
                        natb[:, 0:nchunk * R].rearrange("p (c r) -> p c r", r=R),
                        src_v,
                        waits=[("c_pe", P.cnt["c_pe"]),
                               ("q_sp", P.cnt["q_sp"])])
            nst = None
            for gidx, g0 in enumerate(range(0, nchunk, 8)):
                gsz = min(8, nchunk - g0)
                bi = gidx % 2
                bk = banks[bi]
                lastpe = None
                for j in range(gsz):
                    c = g0 + j
                    w = []
                    if j == 0:
                        w = [("q_pool", nld)] + bankfree[bi]
                        if gidx in (0, 1) and nst is not None:
                            w.append(("q_sp", nst))
                    lastpe = transpose(P, bk[0:16, j * 128:(j + 1) * 128],
                                       natb[:, c * R:(c + 1) * R], ident[:, :],
                                       waits=w, inc="c_pe" if j == gsz - 1 else None)
                nev = P.eop("dve", (lambda h,
                                    o=stage[0:16, g0 * 128:(g0 + gsz) * 128],
                                    i=bk[0:16, 0:gsz * 128]:
                                    h.tensor_scalar_mul(o, i, 2.0)),
                            waits=[("c_pe", lastpe), ("q_sp", P.cnt["q_sp"])])
                bankfree[bi] = [("c_dve", nev)]
            nst = P.dma("sp", dst_dram, stage[0:16, 0:nchunk * 128],
                        waits=[("c_dve", P.cnt["c_dve"])])
            return nst

        prep_bt(b1, b1t_s[0:R, :], NOC)
        prep_bt(b3, b3t_s[0:R, :], NOC)
        prep_bt(b2, b2t_s[0:R, :], NDC)

        P.barrier()
        P.play()

    w1_v = w1.rearrange("(r p) d -> p r d", p=128)
    w3_v = w3.rearrange("(r p) d -> p r d", p=128)
    w2t_v2 = w2t_s.rearrange("(oc p) d -> p oc d", p=128)
    W2P_SEMS = [f"s_w2p{i}" for i in range(8)]

    def ffn(src, dst, tag, fuse_prep=False):
        src_v = src.rearrange("(tt p) d -> p tt d", p=128)
        dst_v = dst.rearrange("(tt p) d -> p tt d", p=128)
        with ExitStack() as es_f:
            gT = es_f.enter_context(nc.sbuf_tensor(tag + "gT", [128, NOC * TB], BF16))
            xr1t = es_f.enter_context(nc.sbuf_tensor(tag + "xr1t", [16, TB], BF16))
            xr3t = es_f.enter_context(nc.sbuf_tensor(tag + "xr3t", [16, TB], BF16))
            xr2t = es_f.enter_context(nc.sbuf_tensor(tag + "xr2t", [16, TB], BF16))
            xrtm = es_f.enter_context(nc.sbuf_tensor(tag + "xrtm", [128, 64], BF16))
            gT_t = gT.rearrange("p (oc t) -> p oc t", t=TB)

            for tb in range(NTB):
                t0 = tb * TB
                sfx = tag + str(tb)
                prep_a = fuse_prep and tb == 0
                prep_b = fuse_prep and tb == 0
                prep_q = fuse_prep and tb == 1
                with ExitStack() as es_a:
                    xnT = es_a.enter_context(
                        nc.sbuf_tensor(sfx + "xnT", [128, NDC * TB], BF16))
                    stats = es_a.enter_context(
                        nc.sbuf_tensor(sfx + "stats", [128, 8], F32))
                    slw1 = es_a.enter_context(
                        nc.sbuf_tensor(sfx + "slw1", [128, NDC * 384], BF16))
                    slw3 = es_a.enter_context(
                        nc.sbuf_tensor(sfx + "slw3", [128, NDC * 384], BF16))
                    s1b = es_a.enter_context(
                        nc.sbuf_tensor(sfx + "s1b", [128, 3 * 512], BF16))
                    bsl = es_a.enter_context(
                        nc.sbuf_tensor(sfx + "bsl", [16, 2 * 384], BF16))
                    trp_ = es_a.enter_context(
                        nc.psum_tensor(sfx + "trp", [128, 1024], BF16))
                    trp2_ = es_a.enter_context(
                        nc.psum_tensor(sfx + "trp2", [128, 1024], BF16))

                    banks = [trp_, trp2_]
                    bankfree = [P.snapw(), P.snapw()]
                    xnT_t = xnT.rearrange("p (dc t) -> p dc t", t=TB)
                    xr_free = [P.snapw()]

                    with ExitStack() as es_ln:
                        xbuf = es_ln.enter_context(
                            nc.sbuf_tensor(sfx + "xbuf", [128, D], F32))
                        xnb = es_ln.enter_context(
                            nc.sbuf_tensor(sfx + "xnb", [128, D], BF16))
                        xrp = es_ln.enter_context(
                            nc.psum_tensor(sfx + "xrp", [128, 512], F32))
                        xbuf_free = P.snapw()
                        xnb_free = P.snapw()
                        stats_free = P.snapw()
                        for tc in range(TC):
                            nx, nsq, nxn = emit_ln_tc(
                                P, src_v[:, tb * TC + tc, :], xbuf, xnb, stats,
                                EPS_AP, xbuf_free, xnb_free, stats_free)
                            lastpe = emit_transpose_fm(P, ident, banks, bankfree,
                                                       xnb, nxn, xnT_t, tc)
                            xbuf_free = [("c_dve", nxn), ("c_act", nsq)]
                            xnb_free = [("c_pe", lastpe)]
                            stats_free = [("c_dve", nxn)]
                        xnT_ready = ("c_dve", P.cnt["c_dve"])

                        def xr_pass(atile, outT, first_waits):
                            for tc in range(TC):
                                for dc in range(NDC):
                                    w = []
                                    if tc == 0 and dc == 0:
                                        w = list(first_waits) + xr_free[0]
                                    mm(P, xrp[:, tc * 16:(tc + 1) * 16],
                                       xnT_t[:, dc, tc * 128:(tc + 1) * 128],
                                       atile[:, dc * R:(dc + 1) * R],
                                       start=(dc == 0), stop=(dc == NDC - 1),
                                       waits=w,
                                       inc="c_pe" if (dc == NDC - 1) else None)
                            npe = P.cnt["c_pe"]
                            nev = cp_dve(P, xrtm[:, 0:64], xrp[:, 0:64],
                                         waits=[("c_pe", npe)] + xr_free[0])
                            lastpe = None
                            for tc in range(TC):
                                w = []
                                if tc == 0:
                                    w = [("c_dve", nev)] + bankfree[0]
                                lastpe = transpose(
                                    P, banks[0][0:16, tc * 128:(tc + 1) * 128],
                                    xrtm[:, tc * 16:(tc + 1) * 16], ident[:, :],
                                    waits=w, inc="c_pe" if tc == TC - 1 else None)
                            nev2 = cp_dve(P, outT[0:16, :], banks[0][0:16, 0:TB],
                                          waits=[("c_pe", lastpe)])
                            bankfree[0] = [("c_dve", nev2)]
                            xr_free[0] = [("c_dve", nev2), ("c_pe", lastpe)]
                            return ("c_dve", nev2)

                        xr1_ready = xr_pass(a1t, xr1t, [xnT_ready])
                        xr3_ready = xr_pass(a3t, xr3t, [xnT_ready])

                    h1ps = [es_a.enter_context(
                        nc.psum_tensor(sfx + f"h1ps{i}", [128, 512], F32))
                        for i in range(3)]
                    h3ps = [es_a.enter_context(
                        nc.psum_tensor(sfx + f"h3ps{i}", [128, 512], F32))
                        for i in range(3)]
                    natw = None
                    if prep_a:
                        natw = es_a.enter_context(
                            nc.sbuf_tensor(sfx + "natw", [128, 8192], BF16))

                    slabfree = {"w1": P.snapw(), "w3": P.snapw()}
                    slabready = {}
                    slabstored = {"w1": None, "w3": None}
                    natfree = [P.snapw()]
                    bslfree = P.snapw()
                    bslready = {}
                    h1free = [P.snapw() for _ in range(3)]
                    h3free = [P.snapw() for _ in range(3)]
                    h1ready = [None] * 3
                    h3ready = [None] * 3
                    s1ready = [None] * 3
                    s1free = [P.snapw() for _ in range(3)]

                    def load_slab(mat, wi):
                        ow = OWS[wi]
                        base = D * OWB[wi]
                        srcf = w1t_s if mat == "w1" else w3t_s
                        buf = slw1 if mat == "w1" else slw3
                        sm = "s_sl1" if mat == "w1" else "s_sl3"
                        n = P.dma("sp", buf[:, 0:NDC * ow],
                                  srcf[base:base + D * ow].rearrange(
                                      "(p f) -> p f", p=128),
                                  waits=slabfree[mat], sem=sm)
                        slabready[(mat, wi)] = [(sm, n)]

                    natfree2 = [P.snapw(), P.snapw()]
                    wseq = [0]

                    def prep_window(mat, wi):
                        ow = OWS[wi]
                        nsub = ow // 128
                        wsrc = w1_v if mat == "w1" else w3_v
                        buf = slw1 if mat == "w1" else slw3
                        bufv = buf.rearrange("p (dc o) -> p dc o", o=ow)
                        smst = "s_psa" if mat == "w1" else "s_psb"
                        r0 = OWB[wi] // 128
                        nlds = {}

                        def issue_load(sub):
                            hi = (wseq[0] + sub) % 2
                            sm = "s_pna" if hi == 0 else "s_pnb"
                            n = P.dma("pool", natw[:, hi * D:(hi + 1) * D],
                                      wsrc[:, r0 + sub, :],
                                      waits=natfree2[hi], sem=sm)
                            nlds[sub] = (sm, n, hi)

                        issue_load(0)
                        if nsub > 1:
                            issue_load(1)
                        first_ev = [True, True]
                        gi = 0
                        for sub in range(nsub):
                            sm, nld, hi = nlds[sub]
                            for g0 in range(0, NDC, 8):
                                bi = gi % 2
                                bk = banks[bi]
                                lastpe = None
                                for j in range(8):
                                    dc = g0 + j
                                    w = []
                                    if j == 0:
                                        w = list(bankfree[bi])
                                        if g0 == 0:
                                            w.append((sm, nld))
                                    lastpe = transpose(
                                        P, bk[:, j * 128:(j + 1) * 128],
                                        natw[:, hi * D + dc * 128:
                                             hi * D + (dc + 1) * 128],
                                        ident[:, :],
                                        waits=w,
                                        inc="c_pe" if j == 7 else None)
                                wv_ = [("c_pe", lastpe)]
                                ei = gi % 2
                                if first_ev[ei]:
                                    wv_ += slabfree[mat]
                                    if slabstored[mat] is not None:
                                        wv_.append(slabstored[mat])
                                    first_ev[ei] = False
                                o_ap = bufv[:, g0:g0 + 8,
                                            sub * 128:(sub + 1) * 128]
                                i_ap = bk[:, 0:1024].rearrange(
                                    "p (g o) -> p g o", o=128)
                                if ei == 0:
                                    nev = cp_dve(P, o_ap, i_ap, waits=wv_)
                                    bankfree[bi] = [("c_dve", nev)]
                                else:
                                    nev = cp_act(P, o_ap, i_ap, waits=wv_)
                                    bankfree[bi] = [("c_act", nev)]
                                gi += 1
                            natfree2[(wseq[0] + sub) % 2] = \
                                [("c_pe", P.cnt["c_pe"])]
                            if sub + 2 < nsub:
                                issue_load(sub + 2)
                        wseq[0] += nsub
                        marks = [("c_dve", P.cnt["c_dve"]),
                                 ("c_act", P.cnt["c_act"])]
                        slabready[(mat, wi)] = marks
                        base = D * OWB[wi]
                        dstf = w1t_s if mat == "w1" else w3t_s
                        dview = dstf[base:base + D * ow].rearrange(
                            "(p f) -> p f", p=128)
                        nst = P.dma("sp", dview, buf[:, 0:NDC * ow],
                                    waits=marks, sem=smst)
                        slabstored[mat] = (smst, nst)

                    def load_bsl(wi):
                        ow = OWS[wi]
                        P.dma("sp", bsl[0:16, 0:ow],
                              b1t_s[0:R, OWB[wi]:OWB[wi] + ow],
                              waits=bslfree, sem="s_bsl")
                        n3 = P.dma("sp", bsl[0:16, 384:384 + ow],
                                   b3t_s[0:R, OWB[wi]:OWB[wi] + ow], sem="s_bsl")
                        bslready[wi] = ("s_bsl", n3)

                    if prep_a:
                        prep_window("w1", 0)
                        load_bsl(0)
                        prep_window("w3", 0)
                    else:
                        load_slab("w1", 0)
                        load_bsl(0)
                        load_slab("w3", 0)

                    for wi in range(len(OWS)):
                        ow = OWS[wi]
                        nob = ow // 128
                        oc0 = OWB[wi] // 128
                        for ob in range(nob):
                            w = list(h1free[ob])
                            if ob == 0:
                                w += slabready[("w1", wi)]
                                w += [bslready[wi], xr1_ready]
                            for dc in range(NDC):
                                mm(P, h1ps[ob][:, 0:512],
                                   slw1[:, dc * ow + ob * 128:
                                        dc * ow + (ob + 1) * 128],
                                   xnT[:, dc * TB:(dc + 1) * TB],
                                   start=(dc == 0), stop=False,
                                   waits=w if dc == 0 else ())
                            n = mm(P, h1ps[ob][:, 0:512],
                                   bsl[0:16, ob * 128:(ob + 1) * 128],
                                   xr1t[0:16, :],
                                   start=False, stop=True, inc="c_pe")
                            h1ready[ob] = ("c_pe", n)
                        slabfree["w1"] = [("c_pe", P.cnt["c_pe"])]
                        if wi + 1 < len(OWS):
                            if prep_a:
                                prep_window("w1", wi + 1)
                            else:
                                load_slab("w1", wi + 1)
                        for ob in range(nob):
                            n = P.eop("act",
                                      (lambda h, o=s1b[:, ob * 512:(ob + 1) * 512],
                                       i=h1ps[ob][:, 0:512]:
                                       h.activation(o, i, AF.Silu)),
                                      waits=[h1ready[ob]] + s1free[ob])
                            s1ready[ob] = ("c_act", n)
                            h1free[ob] = [("c_act", n)]
                        for ob in range(nob):
                            w = list(h3free[ob])
                            if ob == 0:
                                w += slabready[("w3", wi)]
                                w += [xr3_ready]
                            for dc in range(NDC):
                                mm(P, h3ps[ob][:, 0:512],
                                   slw3[:, dc * ow + ob * 128:
                                        dc * ow + (ob + 1) * 128],
                                   xnT[:, dc * TB:(dc + 1) * TB],
                                   start=(dc == 0), stop=False,
                                   waits=w if dc == 0 else ())
                            n = mm(P, h3ps[ob][:, 0:512],
                                   bsl[0:16, 384 + ob * 128:384 + (ob + 1) * 128],
                                   xr3t[0:16, :],
                                   start=False, stop=True, inc="c_pe")
                            h3ready[ob] = ("c_pe", n)
                        slabfree["w3"] = [("c_pe", P.cnt["c_pe"])]
                        bslfree = [("c_pe", P.cnt["c_pe"])]
                        if wi + 1 < len(OWS):
                            if prep_a:
                                prep_window("w3", wi + 1)
                            else:
                                load_slab("w3", wi + 1)
                            load_bsl(wi + 1)
                        for ob in range(nob):
                            n = P.eop("dve",
                                      (lambda h, o=gT_t[:, oc0 + ob, :],
                                       a=s1b[:, ob * 512:(ob + 1) * 512],
                                       i=h3ps[ob][:, 0:512]:
                                       h.tensor_tensor(o, a, i, OP.mult)),
                                      waits=[h3ready[ob], s1ready[ob]])
                            h3free[ob] = [("c_dve", n)]
                            s1free[ob] = [("c_dve", n)]
                    gT_ready = [("c_dve", P.cnt["c_dve"])]

                    for tc in range(TC):
                        for oc in range(NOC):
                            w = []
                            if tc == 0 and oc == 0:
                                w = list(gT_ready) + h1free[0]
                            mm(P, h1ps[0][:, tc * 16:(tc + 1) * 16],
                               gT_t[:, oc, tc * 128:(tc + 1) * 128],
                               a2t[:, oc * R:(oc + 1) * R],
                               start=(oc == 0), stop=(oc == NOC - 1),
                               waits=w, inc="c_pe" if oc == NOC - 1 else None)
                    nev = cp_dve(P, xrtm[:, 0:64], h1ps[0][:, 0:64],
                                 waits=[("c_pe", P.cnt["c_pe"])])
                    lastpe = None
                    for tc in range(TC):
                        w = []
                        if tc == 0:
                            w = [("c_dve", nev)] + bankfree[0]
                        lastpe = transpose(
                            P, banks[0][0:16, tc * 128:(tc + 1) * 128],
                            xrtm[:, tc * 16:(tc + 1) * 16], ident[:, :],
                            waits=w, inc="c_pe" if tc == TC - 1 else None)
                    nxr2 = cp_dve(P, xr2t[0:16, :], banks[0][0:16, 0:TB],
                                  waits=[("c_pe", lastpe)])
                    xr2_ready = ("c_dve", nxr2)

                with ExitStack() as es_b:
                    n_ops = TC if prep_b else 2 * TC
                    w2sl = [es_b.enter_context(
                        nc.sbuf_tensor(sfx + f"w2sl{i}", [128, 4 * 512], BF16))
                        for i in range(2)]
                    xres = [es_b.enter_context(
                        nc.sbuf_tensor(sfx + f"xres{i}", [128, TC * 512], F32))
                        for i in range(2)]
                    b2sl = es_b.enter_context(
                        nc.sbuf_tensor(sfx + "b2sl", [16, 2 * 512], BF16))
                    ops = [es_b.enter_context(
                        nc.psum_tensor(sfx + f"ops{i}", [128, 512], F32))
                        for i in range(n_ops)]
                    if prep_b:
                        natw2 = es_b.enter_context(
                            nc.sbuf_tensor(sfx + "natw2", [128, HID], BF16))
                        w2stg2 = es_b.enter_context(
                            nc.sbuf_tensor(sfx + "w2stg2", [128, HID], BF16))
                        trb = [es_b.enter_context(
                            nc.psum_tensor(sfx + f"trb{i}", [128, 1024], BF16))
                            for i in range(2)]
                        trbfree = [P.snapw(), P.snapw()]
                        natw2free = [P.snapw(), P.snapw()]
                        stg2free = [P.snapw()]
                        w2p_mark = {}
                    if prep_q:
                        natq = es_b.enter_context(
                            nc.sbuf_tensor(sfx + "natq", [128, 5120], BF16))
                        natqfree = [P.snapw()]

                    def prep_w2_chunk(dmc):
                        halves = [(0, 43), (43, 43)]
                        nldh = []
                        for hx, (o0, osz) in enumerate(halves):
                            hi = hx % 2
                            sm = "s_pna" if hi == 0 else "s_pnb"
                            n = P.dma("pool",
                                      natw2[:, hi * 5504:hi * 5504 + osz * 128],
                                      w2[dmc * 128:(dmc + 1) * 128,
                                         o0 * 128:(o0 + osz) * 128],
                                      waits=natw2free[hi], sem=sm)
                            nldh.append((sm, n))
                        first_ev = [True, True]
                        gidx = 0
                        for hx, (o0, osz) in enumerate(halves):
                            sm, nld = nldh[hx]
                            hi = hx % 2
                            for g0 in range(0, osz, 8):
                                gsz = min(8, osz - g0)
                                bi = gidx % 2
                                bk = trb[bi]
                                lastpe = None
                                for j in range(gsz):
                                    oc = o0 + g0 + j
                                    w = []
                                    if j == 0:
                                        w = list(trbfree[bi])
                                        if g0 == 0:
                                            w.append((sm, nld))
                                    lastpe = transpose(
                                        P, bk[:, j * 128:(j + 1) * 128],
                                        natw2[:, hi * 5504 + (g0 + j) * 128:
                                              hi * 5504 + (g0 + j + 1) * 128],
                                        ident[:, :],
                                        waits=w,
                                        inc="c_pe" if j == gsz - 1 else None)
                                ei = gidx % 2
                                wv_ = [("c_pe", lastpe)]
                                if first_ev[ei]:
                                    wv_ += stg2free[0]
                                    first_ev[ei] = False
                                oc_ap = w2stg2[:, (o0 + g0) * 128:
                                               (o0 + g0 + gsz) * 128]
                                if ei == 0:
                                    nev = cp_dve(P, oc_ap, bk[:, 0:gsz * 128],
                                                 waits=wv_)
                                    trbfree[bi] = [("c_dve", nev)]
                                else:
                                    nev = cp_act(P, oc_ap, bk[:, 0:gsz * 128],
                                                 waits=wv_)
                                    trbfree[bi] = [("c_act", nev)]
                                gidx += 1
                            natw2free[hi] = [("c_pe", P.cnt["c_pe"])]
                        smst = W2P_SEMS[dmc % 8]
                        nst = P.dma("sp", w2t_v2[:, :, dmc * 128:(dmc + 1) * 128],
                                    w2stg2[:, :].rearrange(
                                        "p (oc o) -> p oc o", o=128),
                                    waits=[("c_dve", P.cnt["c_dve"]),
                                           ("c_act", P.cnt["c_act"])], sem=smst)
                        w2p_mark[dmc] = (smst, nst)
                        stg2free[0] = [(smst, nst)]

                    def prep_wqe_chunk(dc):
                        nld = P.dma("pool", natq[:, 0:D],
                                    wq[dc * 128:(dc + 1) * 128, :],
                                    waits=natqfree[0], sem="s_pnb")
                        gv = natq[:, 0:D].rearrange("p (hk g dh) -> p hk g dh",
                                                    g=4, dh=128)
                        t1 = natq[:, D:D + KVD].rearrange(
                            "p (hk dh) -> p hk dh", dh=128)
                        t2 = natq[:, 0:KVD].rearrange(
                            "p (hk dh) -> p hk dh", dh=128)
                        out_v = natq[:, KVD:2 * KVD].rearrange(
                            "p (hk dh) -> p hk dh", dh=128)
                        P.eop("dve", (lambda h, o=t1, a=gv[:, :, 0, :],
                                      b_=gv[:, :, 1, :]: h.tensor_add(o, a, b_)),
                              waits=[("s_pnb", nld)])
                        P.eop("dve", (lambda h, o=t2, a=gv[:, :, 2, :],
                                      b_=gv[:, :, 3, :]: h.tensor_add(o, a, b_)))
                        nfold = P.eop("dve", (lambda h, o=out_v, a=t1, b_=t2:
                                              h.tensor_add(o, a, b_)))
                        nst = P.dma("sp", wqe_s[dc * 128:(dc + 1) * 128, :],
                                    natq[:, KVD:2 * KVD],
                                    waits=[("c_dve", nfold)], sem="s_psa")
                        natqfree[0] = [("s_psa", nst)]

                    ofree = [P.snapw() for _ in range(n_ops)]
                    xresfree = [P.snapw(), P.snapw()]
                    xres_ready = [None, None]
                    w2free = [P.snapw(), P.snapw()]
                    w2ready = {}
                    b2free = P.snapw()
                    b2ready = None

                    ocg_list = [(g0, min(4, NOC - g0)) for g0 in range(0, NOC, 4)]

                    def load_w2(dmw, gidx):
                        g0, gsz = ocg_list[gidx]
                        bi = gidx % 2
                        sm = "s_w2a" if bi == 0 else "s_w2b"
                        wl = list(w2free[bi])
                        if prep_b:
                            for r in range(4):
                                wl.append(w2p_mark[4 * dmw + r])
                        n = P.dma("sp",
                                  w2sl[bi][:, 0:gsz * 512].rearrange(
                                      "p (g o) -> p g o", o=512),
                                  w2t_v2[:, g0:g0 + gsz,
                                         dmw * 512:(dmw + 1) * 512],
                                  waits=wl, sem=sm)
                        w2ready[(dmw, gidx)] = (sm, n)

                    def load_xres(dmw):
                        bi = dmw % 2
                        sm = "s_xra" if bi == 0 else "s_xrb"
                        n = P.dma("sp",
                                  xres[bi][:, :].rearrange(
                                      "p (tt o) -> p tt o", o=512),
                                  src_v[:, tb * TC:(tb + 1) * TC,
                                        dmw * 512:(dmw + 1) * 512],
                                  waits=xresfree[bi], sem=sm)
                        xres_ready[bi] = (sm, n)

                    if prep_b:
                        for dmc in range(4):
                            prep_w2_chunk(dmc)
                    nb2 = P.dma("sp", b2sl[0:16, 0:512], b2t_s[0:R, 0:512],
                                waits=b2free, sem="s_b2")
                    b2ready = ("s_b2", nb2)
                    b2_lora_done = [0, 0]
                    load_w2(0, 0)
                    load_xres(0)

                    for dmw in range(NDMW):
                        bi = dmw % 2
                        ob = 0 if prep_b else (dmw % 2) * TC
                        for gidx, (g0, gsz) in enumerate(ocg_list):
                            if gidx + 1 < len(ocg_list):
                                load_w2(dmw, gidx + 1)
                            elif dmw + 1 < NDMW:
                                load_w2(dmw + 1, 0)
                            if prep_b and dmw + 1 < NDMW and \
                                    gidx in (4, 9, 14, 19):
                                prep_w2_chunk(4 * (dmw + 1) + (4, 9, 14, 19).index(gidx))
                            if prep_q and gidx in (4, 9, 14, 19):
                                prep_wqe_chunk(4 * dmw + (4, 9, 14, 19).index(gidx))
                            wsl = w2sl[gidx % 2]
                            for j in range(gsz):
                                oc = g0 + j
                                for tc in range(TC):
                                    w = []
                                    if j == 0 and tc == 0:
                                        w.append(w2ready[(dmw, gidx)])
                                        if gidx == 0:
                                            w += gT_ready
                                            w.append(xr2_ready)
                                            w.append(b2ready)
                                    if gidx == 0 and j == 0:
                                        w = w + ofree[ob + tc]
                                    mm(P, ops[ob + tc][:, 0:512],
                                       gT_t[:, oc, tc * 128:(tc + 1) * 128],
                                       wsl[:, j * 512:(j + 1) * 512],
                                       start=(oc == 0), stop=False,
                                       waits=w,
                                       inc="c_pe" if (j == gsz - 1 and
                                                      tc == TC - 1) else None)
                            w2free[gidx % 2] = [("c_pe", P.cnt["c_pe"])]
                        oready = [None] * TC
                        for tc in range(TC):
                            n = mm(P, ops[ob + tc][:, 0:512],
                                   xr2t[0:16, tc * 128:(tc + 1) * 128],
                                   b2sl[0:16, (dmw % 2) * 512:
                                        (dmw % 2) * 512 + 512],
                                   start=False, stop=True, inc="c_pe")
                            oready[tc] = ("c_pe", n)
                        b2_lora_done[dmw % 2] = P.cnt["c_pe"]
                        for tc in range(TC):
                            n = P.eop("dve",
                                      (lambda h,
                                       o=xres[bi][:, tc * 512:(tc + 1) * 512],
                                       i=ops[ob + tc][:, 0:512],
                                       x_=xres[bi][:, tc * 512:(tc + 1) * 512]:
                                       h.scalar_tensor_tensor(
                                           o, i, 0.5, x_, OP.mult, OP.add)),
                                      waits=[oready[tc], xres_ready[bi]])
                            ofree[ob + tc] = [("c_dve", n)]
                        smst = "s_sta" if bi == 0 else "s_stb"
                        nst = P.dma("sp",
                                    dst_v[:, tb * TC:(tb + 1) * TC,
                                          dmw * 512:(dmw + 1) * 512],
                                    xres[bi][:, :].rearrange(
                                        "p (tt o) -> p tt o", o=512),
                                    waits=[("c_dve", P.cnt["c_dve"])], sem=smst)
                        xresfree[bi] = [(smst, nst)]
                        if dmw + 1 < NDMW:
                            load_xres(dmw + 1)
                            pb2 = (dmw + 1) % 2
                            nb2 = P.dma("sp",
                                        b2sl[0:16, pb2 * 512:pb2 * 512 + 512],
                                        b2t_s[0:R, (dmw + 1) * 512:
                                              (dmw + 2) * 512],
                                        waits=[("c_pe", b2_lora_done[pb2])],
                                        sem="s_b2")
                            b2ready = ("s_b2", nb2)
                    if prep_q:
                        pass
            P.barrier()
            P.play()

    ffn(x_in, x1h, "f1", fuse_prep=True)

    with ExitStack() as es2:
        Qfm = es2.enter_context(nc.sbuf_tensor("Qfm", [128, NH * TOK], BF16))
        VALfm = es2.enter_context(nc.sbuf_tensor("VALfm", [128, NH * TOK], BF16))

        with ExitStack() as es_a:
            xnT = es_a.enter_context(nc.sbuf_tensor("a_xnT", [128, NDC * TB], BF16))
            xbuf = es_a.enter_context(nc.sbuf_tensor("a_xbuf", [128, D], F32))
            xnb = es_a.enter_context(nc.sbuf_tensor("a_xnb", [128, D], BF16))
            stats = es_a.enter_context(nc.sbuf_tensor("a_stats", [128, 8], F32))
            wqsl = [es_a.enter_context(
                nc.sbuf_tensor("a_wqsl0", [128, NDC * 128], BF16))]
            wqsl.append(wqsl[0])
            kvw_b = [es_a.enter_context(
                nc.sbuf_tensor(f"a_kvw{i}", [128, NDC * 512], BF16))
                for i in range(2)]
            kvout = es_a.enter_context(
                nc.sbuf_tensor("a_kvout", [128, TC * 512], BF16))
            kqps = [es_a.enter_context(
                nc.psum_tensor(f"a_kq{i}", [128, 512], F32)) for i in range(TC)]
            trp_ = es_a.enter_context(nc.psum_tensor("a_trp", [128, 1024], BF16))
            trp2_ = es_a.enter_context(nc.psum_tensor("a_trp2", [128, 1024], BF16))

            banks = [trp_, trp2_]
            bankfree = [P.snapw(), P.snapw()]
            xnT_t = xnT.rearrange("p (dc t) -> p dc t", t=TB)

            wqe_v = wqe_s.rearrange("(dc p) f -> p dc f", p=128)
            wk_v = wk.rearrange("(dc p) f -> p dc f", p=128)
            wv_v = wv.rearrange("(dc p) f -> p dc f", p=128)
            kxo_v = kx_own.rearrange("(tt p) f -> p tt f", p=128)
            x1_v = x1h.rearrange("(tt p) d -> p tt d", p=128)

            wq_free0 = [P.snapw()]
            wq_free = {0: None, 1: None}

            class _WQF:
                def __getitem__(self, i):
                    return wq_free0[0]

                def __setitem__(self, i, v):
                    wq_free0[0] = v
            wq_free = _WQF()
            kv_free = [P.snapw(), P.snapw()]
            kvo_free = P.snapw()
            pfree = [P.snapw() for _ in range(TC)]
            xbuf_free = P.snapw()
            xnb_free = P.snapw()
            stats_free = P.snapw()

            for tb in range(NTB):
                t0 = tb * TB
                for tc in range(TC):
                    nx, nsq, nxn = emit_ln_tc(
                        P, x1_v[:, tb * TC + tc, :], xbuf, xnb, stats,
                        EPS_AP, xbuf_free, xnb_free, stats_free)
                    lastpe = emit_transpose_fm(P, ident, banks, bankfree,
                                               xnb, nxn, xnT_t, tc)
                    xbuf_free = [("c_dve", nxn), ("c_act", nsq)]
                    xnb_free = [("c_pe", lastpe)]
                    stats_free = [("c_dve", nxn)]
                xnT_ready = ("c_dve", P.cnt["c_dve"])

                for fc in range(NH):
                    bi = fc % 2
                    smq = "s_wqa" if bi == 0 else "s_wqb"
                    nwq = P.dma("sp",
                                wqsl[bi][:, :].rearrange(
                                    "p (dc f) -> p dc f", f=128),
                                wqe_v[:, :, fc * 128:(fc + 1) * 128],
                                waits=wq_free[bi], sem=smq)
                    tcq = fc % TC
                    lastpe = None
                    for dc in range(NDC):
                        w = []
                        if dc == 0:
                            w = [(smq, nwq), xnT_ready] + pfree[tcq]
                        lastpe = mm(P, kqps[tcq][:, 0:512],
                                    wqsl[bi][:, dc * 128:(dc + 1) * 128],
                                    xnT_t[:, dc, :],
                                    start=(dc == 0), stop=(dc == NDC - 1),
                                    waits=w,
                                    inc="c_pe" if dc == NDC - 1 else None)
                    wq_free[bi] = [("c_pe", lastpe)]
                    nev = cp_dve(P, Qfm[:, fc * TOK + t0:fc * TOK + t0 + TB],
                                 kqps[tcq][:, 0:512], waits=[("c_pe", lastpe)])
                    pfree[tcq] = [("c_dve", nev)]

                kv_jobs = [(wk_v, 0, 0), (wk_v, 0, 1), (wv_v, KVD, 0),
                           (wv_v, KVD, 1)]

                def load_kv(ji):
                    wsrc, colbase, kvw = kv_jobs[ji]
                    bi = ji % 2
                    sm = "s_kv" if bi == 0 else "s_wo"
                    n = P.dma("pool",
                              kvw_b[bi][:, :].rearrange(
                                  "p (dc f) -> p dc f", f=512),
                              wsrc[:, :, kvw * 512:(kvw + 1) * 512],
                              waits=kv_free[bi], sem=sm)
                    kv_ready[ji] = (sm, n)

                kv_ready = {}
                load_kv(0)
                load_kv(1)
                for ji, (wsrc, colbase, kvw) in enumerate(kv_jobs):
                    if True:
                        bi = ji % 2
                        oready = [None] * TC
                        for tc in range(TC):
                            lastpe = None
                            for dc in range(NDC):
                                w = []
                                if dc == 0:
                                    w = [kv_ready[ji], xnT_ready] + pfree[tc]
                                lastpe = mm(
                                    P, kqps[tc][:, 0:512],
                                    xnT_t[:, dc, tc * 128:(tc + 1) * 128],
                                    kvw_b[bi][:, dc * 512:(dc + 1) * 512],
                                    start=(dc == 0), stop=(dc == NDC - 1),
                                    waits=w,
                                    inc="c_pe" if dc == NDC - 1 else None)
                            oready[tc] = ("c_pe", lastpe)
                        kv_free[bi] = [("c_pe", P.cnt["c_pe"])]
                        if ji + 2 < 4:
                            load_kv(ji + 2)
                        for tc in range(TC):
                            w = [oready[tc]]
                            if tc == 0:
                                w += kvo_free
                            nev = cp_dve(P, kvout[:, tc * 512:(tc + 1) * 512],
                                         kqps[tc][:, 0:512], waits=w)
                            pfree[tc] = [("c_dve", nev)]
                        nst = P.dma("sp",
                                    kxo_v[:, tb * TC:(tb + 1) * TC,
                                          colbase + kvw * 512:
                                          colbase + (kvw + 1) * 512],
                                    kvout[:, :].rearrange(
                                        "p (tt o) -> p tt o", o=512),
                                    waits=[("c_dve", P.cnt["c_dve"])],
                                    sem="s_kst")
                        kvo_free = [("s_kst", nst)]

                P.wait("pool", "s_kst", P.cnt["s_kst"])

                def cc_fn(h, tb=tb):
                    return h.collective_compute(
                        "AllGather", OP.bypass,
                        replica_groups=[[0, 1], [2, 3], [4, 5], [6, 7]],
                        ins=[kx_own[tb * TB:(tb + 1) * TB, :]],
                        outs=[kx_all[tb * 2 * TB:(tb + 1) * 2 * TB, :]])
                P.op("pool", cc_fn, inc="cc")

            P.barrier()
            P.play()

        with ExitStack() as es_d:
            Kfm = es_d.enter_context(nc.sbuf_tensor("Kfm", [128, NH * S], BF16))
            ktmp = [es_d.enter_context(
                nc.sbuf_tensor(f"d_ktmp{i}", [128, KVD], BF16))
                for i in range(2)]
            Vh = [es_d.enter_context(
                nc.sbuf_tensor(f"d_Vh{i}", [128, S], BF16)) for i in range(2)]
            Pb = [es_d.enter_context(
                nc.sbuf_tensor(f"d_P{i}", [128, S], BF16)) for i in range(2)]
            PT = es_d.enter_context(nc.sbuf_tensor("d_PT", [128, NKC * 512], BF16))
            sums = es_d.enter_context(nc.sbuf_tensor("d_sums", [128, 8], F32))
            scps = es_d.enter_context(nc.psum_tensor("d_sc", [128, 2048], F32))
            pvps = es_d.enter_context(nc.psum_tensor("d_pv", [128, 512], F32))
            trp_ = es_d.enter_context(nc.psum_tensor("d_trp", [128, 1024], BF16))
            trp2_ = es_d.enter_context(nc.psum_tensor("d_trp2", [128, 1024], BF16))

            banks = [trp_, trp2_]
            bankfree = [P.snapw(), P.snapw()]
            NRND = max(1, NKB // 4)
            RW = min(NKB, 4) * 512

            P.wait("sp", "cc", P.cnt["cc"])
            Kfm_t = Kfm.rearrange("p (hh k) -> p hh k", k=S)
            ktfree = [P.snapw(), P.snapw()]
            ktld = {}

            def load_kt(kc):
                tbi = kc % 2
                sm = "s_kt" if tbi == 0 else "s_vh"
                n = P.dma("sp", ktmp[tbi][:, :],
                          kx_all[kc * 128:(kc + 1) * 128, 0:KVD],
                          waits=ktfree[tbi], sem=sm)
                ktld[kc] = (sm, n)

            load_kt(0)
            load_kt(1)
            for kc in range(NKC):
                tbi = kc % 2
                bi = kc % 2
                bk = banks[bi]
                lastpe = None
                for hh in range(NH):
                    w = []
                    if hh == 0:
                        w = [ktld[kc]] + bankfree[bi]
                    lastpe = transpose(P, bk[:, hh * 128:(hh + 1) * 128],
                                       ktmp[tbi][:, hh * 128:(hh + 1) * 128],
                                       ident[:, :],
                                       waits=w, inc="c_pe" if hh == NH - 1 else None)
                ktfree[tbi] = [("c_pe", lastpe)]
                if kc + 2 < NKC:
                    load_kt(kc + 2)
                nev = cp_dve(P, Kfm_t[:, :, kc * 128:(kc + 1) * 128],
                             bk[:, :].rearrange("p (j o) -> p j o", o=128),
                             waits=[("c_pe", lastpe)])
                bankfree[bi] = [("c_dve", nev)]
            kfm_ready = ("c_dve", P.cnt["c_dve"])

            kx_v = kx_all.rearrange("(rk p) f -> p rk f", p=128)
            PT_t = PT.rearrange("p (kc q) -> p kc q", q=512)

            scfree = [kfm_ready] + P.snapw()
            pvfree = P.snapw()
            vh_free = [P.snapw(), P.snapw()]
            vh_ready = {}

            def load_vh(hh):
                vbi = hh % 2
                sm = "s_vh" if vbi == 0 else "s_kt"
                n = P.dma("sp",
                          Vh[vbi][:, :].rearrange("p (kc o) -> p kc o", o=128),
                          kx_v[:, 0:NKC, KVD + hh * 128:KVD + (hh + 1) * 128],
                          waits=vh_free[vbi], sem=sm)
                vh_ready[hh] = (sm, n)
            sums_free = [P.snapw(), P.snapw()]
            pbfree = [P.snapw(), P.snapw()]
            ptfree = [P.snapw()]

            def emit_tail(qt, nexp_marks):
                par = qt % 2
                nred = P.eop("dve", (lambda h, o=sums[:, par * 4 + 2:par * 4 + 3],
                                     i=sums[:, par * 4:par * 4 + NRND]:
                                     h.reduce_sum(o, i, axis=AX)),
                             waits=nexp_marks)
                nrec = P.eop("dve", (lambda h, o=sums[:, par * 4 + 3:par * 4 + 4],
                                     i=sums[:, par * 4 + 2:par * 4 + 3]:
                                     h.reciprocal(o, i)))
                sums_free[par] = [("c_dve", nrec)]
                nnorm = P.eop("dve", (lambda h, o=Pb[par][:, :], i=Pb[par][:, :],
                                      s_=sums[:, par * 4 + 3:par * 4 + 4]:
                                      h.tensor_scalar_mul(o, i, s_)))
                lastpe = None
                for gidx, g0 in enumerate(range(0, NKC, 8)):
                    bi = gidx % 2
                    bk = banks[bi]
                    for j in range(8):
                        kc = g0 + j
                        w = []
                        if j == 0:
                            w = [("c_dve", nnorm)] + bankfree[bi]
                        lastpe = transpose(
                            P, bk[:, j * 128:(j + 1) * 128],
                            Pb[par][:, kc * 128:(kc + 1) * 128], ident[:, :],
                            waits=w, inc="c_pe" if j == 7 else None)
                    wv_ = [("c_pe", lastpe)]
                    if gidx < 2:
                        wv_ += ptfree[0]
                    nev = cp_dve(P, PT_t[:, g0:g0 + 8,
                                         qt * 128:(qt + 1) * 128],
                                 bk[:, :].rearrange("p (j o) -> p j o", o=128),
                                 waits=wv_)
                    bankfree[bi] = [("c_dve", nev)]
                pbfree[par] = [("c_pe", lastpe)]

            load_vh(0)
            load_vh(1)
            for hh in range(NH):
                vbi = hh % 2
                for qg in range(TOK // 512):
                    tails = []
                    for qt in range(TC):
                        par = qt % 2
                        q0 = qg * 512 + qt * 128
                        nexp_marks = []
                        for rnd in range(NRND):
                            lastmm = None
                            for j in range(RW // 512):
                                kb = rnd * 4 + j
                                w = []
                                if j == 0:
                                    w = list(scfree)
                                lastmm = mm(
                                    P, scps[:, j * 512:(j + 1) * 512],
                                    Qfm[:, hh * TOK + q0:hh * TOK + q0 + 128],
                                    Kfm_t[:, hh, kb * 512:(kb + 1) * 512],
                                    start=True, stop=True,
                                    waits=w,
                                    inc="c_pe" if j == RW // 512 - 1 else None)
                            wexp = [("c_pe", lastmm)] + sums_free[par]
                            if rnd == 0:
                                wexp += pbfree[par]
                            nex = P.eop("act",
                                        (lambda h,
                                         o=Pb[par][:, rnd * RW:(rnd + 1) * RW],
                                         i=scps[:, 0:RW],
                                         b_=SMB_AP, s_=SMS_AP,
                                         a=sums[:, par * 4 + rnd:
                                                par * 4 + rnd + 1]:
                                         h.activation(o, i, AF.Exp, bias=b_,
                                                      scale=s_, accum_out=a)),
                                        waits=wexp)
                            nexp_marks.append(("c_act", nex))
                            scfree = [("c_act", nex)]
                        tails.append((qt, nexp_marks))
                        if len(tails) > 1:
                            emit_tail(*tails.pop(0))
                    emit_tail(*tails.pop(0))
                    pt_ready = ("c_dve", P.cnt["c_dve"])
                    lastpe = None
                    for kc in range(NKC):
                        w = []
                        if kc == 0:
                            w = [pt_ready, vh_ready[hh]] + pvfree
                        lastpe = mm(P, pvps[:, 0:512],
                                    Vh[vbi][:, kc * 128:(kc + 1) * 128],
                                    PT_t[:, kc, :],
                                    start=(kc == 0), stop=(kc == NKC - 1),
                                    waits=w,
                                    inc="c_pe" if kc == NKC - 1 else None)
                    ptfree[0] = [("c_pe", lastpe)]
                    nev = cp_dve(P, VALfm[:, hh * TOK + qg * 512:
                                          hh * TOK + (qg + 1) * 512],
                                 pvps[:, 0:512], waits=[("c_pe", lastpe)])
                    pvfree = [("c_dve", nev)]
                vh_free[vbi] = [("c_pe", P.cnt["c_pe"])]
                if hh + 2 < NH:
                    load_vh(hh + 2)
            val_ready = ("c_dve", P.cnt["c_dve"])
            P.barrier()
            P.play()

        with ExitStack() as es_e:
            wob = [es_e.enter_context(
                nc.sbuf_tensor(f"e_wob{i}", [128, NH * 512], BF16))
                for i in range(2)]
            xres = [es_e.enter_context(
                nc.sbuf_tensor(f"e_xres{i}", [128, TC * 512], F32))
                for i in range(2)]
            ops = [es_e.enter_context(
                nc.psum_tensor(f"e_ops{i}", [128, 512], F32)) for i in range(4)]

            wo_v = wo.rearrange("(kc p) d -> p kc d", p=128)
            x1_v2 = x1h.rearrange("(tt p) d -> p tt d", p=128)
            x2_v = x2h.rearrange("(tt p) d -> p tt d", p=128)

            wofree = [P.snapw(), P.snapw()]
            ofree = [P.snapw() for _ in range(4)]
            xresfree = [P.snapw(), P.snapw()]
            wo_jobs = [(tg, dmw) for tg in range(TOK // 512)
                       for dmw in range(NDMW)]
            wo_ready = {}

            def load_wo(ji):
                _, dmw = wo_jobs[ji]
                bi = ji % 2
                sm = "s_wo" if bi == 0 else "s_kv"
                n = P.dma("pool",
                          wob[bi][:, :].rearrange("p (kc o) -> p kc o", o=512),
                          wo_v[:, :, dmw * 512:(dmw + 1) * 512],
                          waits=wofree[bi], sem=sm)
                wo_ready[ji] = (sm, n)

            load_wo(0)
            load_wo(1)
            for ji, (tg, dmw) in enumerate(wo_jobs):
                if True:
                    wbi = ji % 2
                    woready = wo_ready[ji]
                    bi = dmw % 2
                    smx = "s_xra" if bi == 0 else "s_xrb"
                    nxr = P.dma("sp",
                                xres[bi][:, :].rearrange(
                                    "p (tt o) -> p tt o", o=512),
                                x1_v2[:, tg * TC:(tg + 1) * TC,
                                      dmw * 512:(dmw + 1) * 512],
                                waits=xresfree[bi], sem=smx)
                    oready = [None] * 4
                    for tt in range(4):
                        lastpe = None
                        for kvc in range(NH):
                            w = []
                            if kvc == 0:
                                w = [woready, val_ready] + ofree[tt]
                            lastpe = mm(
                                P, ops[tt][:, 0:512],
                                VALfm[:, kvc * TOK + tg * 512 + tt * 128:
                                      kvc * TOK + tg * 512 + (tt + 1) * 128],
                                wob[wbi][:, kvc * 512:(kvc + 1) * 512],
                                start=(kvc == 0), stop=(kvc == NH - 1),
                                waits=w, inc="c_pe" if kvc == NH - 1 else None)
                        oready[tt] = ("c_pe", lastpe)
                    wofree[wbi] = [("c_pe", P.cnt["c_pe"])]
                    if ji + 2 < len(wo_jobs):
                        load_wo(ji + 2)
                    for tt in range(4):
                        n = P.eop("dve",
                                  (lambda h,
                                   o=xres[bi][:, tt * 512:(tt + 1) * 512],
                                   i=ops[tt][:, 0:512],
                                   x_=xres[bi][:, tt * 512:(tt + 1) * 512]:
                                   h.tensor_tensor(o, i, x_, OP.add)),
                                  waits=[oready[tt], (smx, nxr)])
                        ofree[tt] = [("c_dve", n)]
                    smst = "s_sta" if bi == 0 else "s_stb"
                    nst = P.dma("sp",
                                x2_v[:, tg * TC:(tg + 1) * TC,
                                     dmw * 512:(dmw + 1) * 512],
                                xres[bi][:, :].rearrange(
                                    "p (tt o) -> p tt o", o=512),
                                waits=[("c_dve", P.cnt["c_dve"])], sem=smst)
                    xresfree[bi] = [(smst, nst)]
            P.barrier()
            P.play()

    ffn(x2h, y_out, "f3")

    for smn in SNAP_SEMS:
        P.wait("sp", smn, P.cnt[smn])
    P.play()

    es_top.close()
    return nc



def kernel(L0_attn_x, L0_attn_wq, L0_attn_wk, L0_attn_wv, L0_attn_wo,
           L0_ffn_w1, L0_ffn_w3, L0_ffn_w2,
           L0_ffn_lora_A1, L0_ffn_lora_B1, L0_ffn_lora_A3, L0_ffn_lora_B3,
           L0_ffn_lora_A2, L0_ffn_lora_B2):
    x = np.ascontiguousarray(np.asarray(L0_attn_x, dtype=np.float32))
    x2d = x.reshape(B * S, D)
    ident = np.eye(128, dtype=ml_dtypes.bfloat16)
    consts = np.tile(np.array([[EPS, SM_BIAS, SM_SCALE, 0.0]], dtype=np.float32),
                     (128, 1))
    common = {
        "wq": np.ascontiguousarray(np.asarray(L0_attn_wq, dtype=np.float32)),
        "wk": np.ascontiguousarray(np.asarray(L0_attn_wk, dtype=np.float32)),
        "wv": np.ascontiguousarray(np.asarray(L0_attn_wv, dtype=np.float32)),
        "wo": np.ascontiguousarray(np.asarray(L0_attn_wo, dtype=np.float32)),
        "w1": np.ascontiguousarray(np.asarray(L0_ffn_w1, dtype=np.float32)),
        "w3": np.ascontiguousarray(np.asarray(L0_ffn_w3, dtype=np.float32)),
        "w2": np.ascontiguousarray(np.asarray(L0_ffn_w2, dtype=np.float32)),
        "a1": np.ascontiguousarray(np.asarray(L0_ffn_lora_A1, dtype=np.float32)),
        "b1": np.ascontiguousarray(np.asarray(L0_ffn_lora_B1, dtype=np.float32)),
        "a3": np.ascontiguousarray(np.asarray(L0_ffn_lora_A3, dtype=np.float32)),
        "b3": np.ascontiguousarray(np.asarray(L0_ffn_lora_B3, dtype=np.float32)),
        "a2": np.ascontiguousarray(np.asarray(L0_ffn_lora_A2, dtype=np.float32)),
        "b2": np.ascontiguousarray(np.asarray(L0_ffn_lora_B2, dtype=np.float32)),
        "identw": ident,
        "consts": consts,
    }
    in_maps = []
    for c in range(NCORE):
        m = dict(common)
        m["x_in"] = np.ascontiguousarray(x2d[c * TOK:(c + 1) * TOK, :])
        in_maps.append(m)

    nc = build_program()
    res = run_bass_kernel_spmd(nc, in_maps, core_ids=list(range(NCORE)))
    y = np.concatenate([res.results[c]["y_out"] for c in range(NCORE)], axis=0)
    return y.reshape(B, S, D)


if __name__ == "__main__":
    import time
    t0 = time.time()
    nc = build_program()
    print(f"build ok in {time.time()-t0:.1f}s")


# version 08
# speedup vs baseline: 1.2627x; 1.2352x over previous
import numpy as np
import ml_dtypes
from contextlib import ExitStack

import concourse.bass as bass
import concourse.mybir as mybir
from concourse.bass_utils import run_bass_kernel_spmd

F32 = mybir.dt.float32
BF16 = mybir.dt.bfloat16
AF = mybir.ActivationFunctionType
OP = mybir.AluOpType
AX = mybir.AxisListType.X

D = 4096
HID = 11008
KVD = 1024
NH = 8
R = 16
B = 4
S = 4096
NCORE = 8
TOK = 2048
TB = 512
NTB = TOK // TB
TC = TB // 128
NDC = D // 128
NOC = HID // 128
OWS = [384] * 28 + [256]
OWB = [0]
for _w in OWS:
    OWB.append(OWB[-1] + _w)
DMW = 512
NDMW = D // DMW
NKC = S // 128
NKB = S // 512
EPS = 1e-5
SM_SCALE = 1.0 / float(np.sqrt(128.0))
SM_BIAS = -12.0

ENGS = ("pe", "act", "dve", "pool", "sp")
SEM_NAMES = ["c_pe", "c_act", "c_dve", "c_pool", "q_sp", "q_pool", "cc",
             "s_ln", "s_sl1", "s_sl3", "s_bsl", "s_w2a", "s_w2b",
             "s_xra", "s_xrb", "s_sta", "s_stb", "s_b2",
             "s_wqa", "s_wqb", "s_kv", "s_kst", "s_kt", "s_vh", "s_wo",
             "s_pna", "s_pnb", "s_psa", "s_psb", "s_w2c", "s_bsl2", "s_b22"] + [f"s_w2p{i}" for i in range(8)]
SNAP_SEMS = [n for n in SEM_NAMES if n != "cc"]

def configure(tok, seq):
    global TOK, S, NTB, NKC, NKB
    TOK = tok
    S = seq
    NTB = TOK // TB
    NKC = S // 128
    NKB = S // 512


class Plan:

    def __init__(self, nc, semh):
        self.nc = nc
        self.semh = semh
        self.items = {e: [] for e in ENGS}
        self.cnt = {name: 0 for name in semh}

    def snapw(self):
        return [(s, self.cnt[s]) for s in SNAP_SEMS if self.cnt[s] > 0]

    def wait(self, eng, sem, val):
        if val <= 0:
            return
        assert val <= self.cnt[sem], \
            f"wait on future count: {sem} {val} > {self.cnt[sem]}"
        self.items[eng].append(("w", sem, val))

    def waitl(self, eng, waits):
        for (s, v) in waits:
            self.wait(eng, s, v)

    def op(self, eng, fn, inc=None, inc_k=1):
        n = None
        if inc is not None:
            self.cnt[inc] += inc_k
            n = self.cnt[inc]
        self.items[eng].append(("o", fn, inc, inc_k))
        return n

    def eop(self, eng, fn, waits=()):
        ch = "c_" + eng
        self.waitl(eng, waits)
        self.wait(eng, ch, self.cnt[ch])
        return self.op(eng, fn, inc=ch)

    def dma(self, q, out, in_, waits=(), sem=None):
        if sem is None:
            sem = "q_sp" if q == "sp" else "q_pool"
        self.waitl(q, waits)

        def fn(h, out=out, in_=in_):
            return h.dma_start(out=out, in_=in_)
        return self.op(q, fn, inc=sem, inc_k=16)

    def barrier(self):
        snap = {s: self.cnt[s] for s in self.cnt}
        for e in ENGS:
            for s, v in snap.items():
                if v > 0:
                    self.items[e].append(("w", s, v))

    def play(self):
        nc = self.nc

        def runner(eng):
            items = self.items[eng]
            semh = self.semh

            def go(h):
                for it in items:
                    if it[0] == "w":
                        h.wait_ge(semh[it[1]], it[2])
                    else:
                        _, fn, inc, k = it
                        ins = fn(h)
                        if inc is not None:
                            ins.then_inc(semh[inc], k)
            return go

        with nc.Block() as block:
            block.tensor(runner("pe"))
            block.scalar(runner("act"))
            block.vector(runner("dve"))
            block.gpsimd(runner("pool"))
            block.sync(runner("sp"))
        self.items = {e: [] for e in ENGS}


def mm(P, out_ap, lhsT_ap, rhs_ap, start, stop, waits=(), inc=None):
    P.waitl("pe", waits)

    def fn(h, o=out_ap, l=lhsT_ap, r=rhs_ap, st=start, sp=stop):
        return h.matmul(o, l, r, start=st, stop=sp)
    return P.op("pe", fn, inc=inc)


def transpose(P, out_ap, in_ap, ident_ap, waits=(), inc=None):
    P.waitl("pe", waits)

    def fn(h, o=out_ap, i=in_ap, d=ident_ap):
        return h.transpose(o, i, d)
    return P.op("pe", fn, inc=inc)


def cp_dve(P, out_ap, in_ap, waits=()):
    return P.eop("dve", (lambda h, o=out_ap, i=in_ap: h.tensor_copy(o, i)),
                 waits=waits)


def cp_act(P, out_ap, in_ap, waits=()):
    return P.eop("act", (lambda h, o=out_ap, i=in_ap: h.copy(o, i)),
                 waits=waits)


def emit_ln_tc(P, src_row_ap, xbuf, xnb, stats, eps_ap, xbuf_free, xnb_free,
               stats_free):
    nx = P.dma("sp", xbuf[:, :], src_row_ap, waits=xbuf_free, sem="s_ln")
    P.eop("dve", (lambda h, o=stats[:, 0:1], i=xbuf[:, :]:
                  h.reduce_sum(o, i, axis=AX)),
          waits=[("s_ln", nx)] + stats_free)
    nsq = P.eop("act", (lambda h, o=xnb[:, :], i=xbuf[:, :], a=stats[:, 3:4]:
                        h.activation(o, i, AF.Square, accum_out=a)),
                waits=[("s_ln", nx)] + xnb_free + stats_free)
    P.eop("dve", (lambda h, o=stats[:, 1:2], i=stats[:, 0:1]:
                  h.tensor_scalar_mul(o, i, -1.0 / D)))
    P.eop("dve", (lambda h, o=stats[:, 2:3], i=stats[:, 1:2]:
                  h.tensor_mul(o, i, i)))
    nvar = P.eop("dve", (lambda h, o=stats[:, 4:5], i=stats[:, 3:4],
                         m=stats[:, 2:3]:
                         h.scalar_tensor_tensor(o, i, 1.0 / D, m,
                                                OP.mult, OP.subtract)),
                 waits=[("c_act", nsq)])
    nstd = P.eop("act", (lambda h, o=stats[:, 5:6], i=stats[:, 4:5], b_=eps_ap:
                         h.activation(o, i, AF.Sqrt, bias=b_)),
                 waits=[("c_dve", nvar)])
    P.eop("dve", (lambda h, o=stats[:, 6:7], i=stats[:, 5:6]:
                  h.reciprocal(o, i)),
          waits=[("c_act", nstd)])
    nxn = P.eop("dve", (lambda h, o=xnb[:, :], i=xbuf[:, :],
                        s1=stats[:, 1:2], s2=stats[:, 6:7]:
                        h.tensor_scalar(o, i, s1, s2, OP.add, OP.mult)),
                waits=[("c_act", nsq)])
    return nx, nsq, nxn


def emit_transpose_fm(P, ident, banks, bankfree, xnb, nxn, xnT_t, tc):
    lastpe = None
    for gidx, g0 in enumerate(range(0, NDC, 8)):
        bi = gidx % 2
        bk = banks[bi]
        for j in range(8):
            dc = g0 + j
            w = []
            if j == 0:
                w = [("c_dve", nxn)] + bankfree[bi]
            lastpe = transpose(P, bk[:, j * 128:(j + 1) * 128],
                               xnb[:, dc * 128:(dc + 1) * 128], ident[:, :],
                               waits=w, inc="c_pe" if j == 7 else None)
        nev = cp_dve(P, xnT_t[:, g0:g0 + 8, tc * 128:(tc + 1) * 128],
                     bk[:, :].rearrange("p (j o) -> p j o", o=128),
                     waits=[("c_pe", lastpe)])
        bankfree[bi] = [("c_dve", nev)]
    return lastpe


def build_program():
    nc = bass.Bass("TRN2", target_bir_lowering=False)

    x_in = nc.dram_tensor("x_in", [TOK, D], F32, kind="ExternalInput").ap()
    wq = nc.dram_tensor("wq", [D, D], F32, kind="ExternalInput").ap()
    wk = nc.dram_tensor("wk", [D, KVD], F32, kind="ExternalInput").ap()
    wv = nc.dram_tensor("wv", [D, KVD], F32, kind="ExternalInput").ap()
    wo = nc.dram_tensor("wo", [KVD, D], F32, kind="ExternalInput").ap()
    w1 = nc.dram_tensor("w1", [HID, D], F32, kind="ExternalInput").ap()
    w3 = nc.dram_tensor("w3", [HID, D], F32, kind="ExternalInput").ap()
    w2 = nc.dram_tensor("w2", [D, HID], F32, kind="ExternalInput").ap()
    a1 = nc.dram_tensor("a1", [R, D], F32, kind="ExternalInput").ap()
    b1 = nc.dram_tensor("b1", [HID, R], F32, kind="ExternalInput").ap()
    a3 = nc.dram_tensor("a3", [R, D], F32, kind="ExternalInput").ap()
    b3 = nc.dram_tensor("b3", [HID, R], F32, kind="ExternalInput").ap()
    a2 = nc.dram_tensor("a2", [R, HID], F32, kind="ExternalInput").ap()
    b2 = nc.dram_tensor("b2", [D, R], F32, kind="ExternalInput").ap()
    identw = nc.dram_tensor("identw", [128, 128], BF16, kind="ExternalInput").ap()
    consts = nc.dram_tensor("consts", [128, 4], F32, kind="ExternalInput").ap()
    y_out = nc.dram_tensor("y_out", [TOK, D], F32, kind="ExternalOutput").ap()

    x1h = nc.dram_tensor("x1h", [TOK, D], F32, kind="Internal").ap()
    x2h = nc.dram_tensor("x2h", [TOK, D], F32, kind="Internal").ap()
    w1t_s = nc.dram_tensor("w1t_s", [D * HID], BF16, kind="Internal").ap()
    w3t_s = nc.dram_tensor("w3t_s", [D * HID], BF16, kind="Internal").ap()
    w2t_s = nc.dram_tensor("w2t_s", [HID, D], BF16, kind="Internal").ap()
    wqe_s = nc.dram_tensor("wqe_s", [D, KVD], BF16, kind="Internal").ap()
    b1t_s = nc.dram_tensor("b1t_s", [R, HID], BF16, kind="Internal").ap()
    b3t_s = nc.dram_tensor("b3t_s", [R, HID], BF16, kind="Internal").ap()
    b2t_s = nc.dram_tensor("b2t_s", [R, D], BF16, kind="Internal").ap()
    kx_own = nc.dram_tensor("kx_own", [TOK, 2 * KVD], BF16, kind="Internal").ap()
    kx_all = nc.dram_tensor("kx_all", [S, 2 * KVD], BF16, kind="Internal").ap()

    es_top = ExitStack()
    semh = {name: es_top.enter_context(nc.semaphore(name)) for name in SEM_NAMES}

    ident = es_top.enter_context(nc.sbuf_tensor("ident", [128, 128], BF16))
    csb = es_top.enter_context(nc.sbuf_tensor("csb", [128, 4], F32))
    a1t = es_top.enter_context(nc.sbuf_tensor("a1t", [128, NDC * R], BF16))
    a3t = es_top.enter_context(nc.sbuf_tensor("a3t", [128, NDC * R], BF16))
    a2t = es_top.enter_context(nc.sbuf_tensor("a2t", [128, NOC * R], BF16))

    P = Plan(nc, semh)
    EPS_AP = csb[:, 0:1]
    SMB_AP = csb[:, 1:2]
    SMS_AP = csb[:, 2:3]

    with ExitStack() as es:
        natb = es.enter_context(nc.sbuf_tensor("p0nat", [128, NOC * R], BF16))
        stage = es.enter_context(nc.sbuf_tensor("p0stg", [16, NOC * 128], BF16))
        anat = es.enter_context(nc.sbuf_tensor("p0anat", [16, NOC * 128], BF16))
        trp_ = es.enter_context(nc.psum_tensor("p0trp", [128, 1024], BF16))
        trp2_ = es.enter_context(nc.psum_tensor("p0trp2", [128, 1024], BF16))
        banks = [trp_, trp2_]
        bankfree = [[], []]

        P.dma("sp", ident[:, :], identw)
        P.dma("sp", csb[:, :], consts)
        P.wait("pe", "q_sp", P.cnt["q_sp"])

        def prep_at(a_src, dst, nchunk):
            nld = P.dma("pool", anat[0:16, 0:nchunk * 128], a_src,
                        waits=[("c_pe", P.cnt["c_pe"])])
            for gidx, g0 in enumerate(range(0, nchunk, 8)):
                gsz = min(8, nchunk - g0)
                bi = gidx % 2
                bk = banks[bi]
                lastpe = None
                for j in range(gsz):
                    c = g0 + j
                    w = []
                    if j == 0:
                        w = [("q_pool", nld)] + bankfree[bi]
                    lastpe = transpose(P, bk[:, j * 16:(j + 1) * 16],
                                       anat[0:16, c * 128:(c + 1) * 128],
                                       ident[0:16, 0:16],
                                       waits=w, inc="c_pe" if j == gsz - 1 else None)
                nev = cp_dve(P, dst[:, g0 * R:(g0 + gsz) * R],
                             bk[:, 0:gsz * R], waits=[("c_pe", lastpe)])
                bankfree[bi] = [("c_dve", nev)]

        prep_at(a1, a1t, NDC)
        prep_at(a3, a3t, NDC)
        prep_at(a2, a2t, NOC)

        def prep_bt(b_src, dst_dram, nchunk):
            src_v = b_src.rearrange("(c p) r -> p c r", p=128)
            nld = P.dma("pool",
                        natb[:, 0:nchunk * R].rearrange("p (c r) -> p c r", r=R),
                        src_v,
                        waits=[("c_pe", P.cnt["c_pe"]),
                               ("q_sp", P.cnt["q_sp"])])
            nst = None
            for gidx, g0 in enumerate(range(0, nchunk, 8)):
                gsz = min(8, nchunk - g0)
                bi = gidx % 2
                bk = banks[bi]
                lastpe = None
                for j in range(gsz):
                    c = g0 + j
                    w = []
                    if j == 0:
                        w = [("q_pool", nld)] + bankfree[bi]
                        if gidx in (0, 1) and nst is not None:
                            w.append(("q_sp", nst))
                    lastpe = transpose(P, bk[0:16, j * 128:(j + 1) * 128],
                                       natb[:, c * R:(c + 1) * R], ident[:, :],
                                       waits=w, inc="c_pe" if j == gsz - 1 else None)
                nev = P.eop("dve", (lambda h,
                                    o=stage[0:16, g0 * 128:(g0 + gsz) * 128],
                                    i=bk[0:16, 0:gsz * 128]:
                                    h.tensor_scalar_mul(o, i, 2.0)),
                            waits=[("c_pe", lastpe), ("q_sp", P.cnt["q_sp"])])
                bankfree[bi] = [("c_dve", nev)]
            nst = P.dma("sp", dst_dram, stage[0:16, 0:nchunk * 128],
                        waits=[("c_dve", P.cnt["c_dve"])])
            return nst

        prep_bt(b1, b1t_s[0:R, :], NOC)
        prep_bt(b3, b3t_s[0:R, :], NOC)
        prep_bt(b2, b2t_s[0:R, :], NDC)

        P.barrier()
        P.play()

    w1_v = w1.rearrange("(r p) d -> p r d", p=128)
    w3_v = w3.rearrange("(r p) d -> p r d", p=128)
    w2t_v2 = w2t_s.rearrange("(oc p) d -> p oc d", p=128)
    W2P_SEMS = [f"s_w2p{i}" for i in range(8)]

    def ffn(src, dst, tag, fuse_prep=False):
        src_v = src.rearrange("(tt p) d -> p tt d", p=128)
        dst_v = dst.rearrange("(tt p) d -> p tt d", p=128)
        with ExitStack() as es_f:
            gT = es_f.enter_context(nc.sbuf_tensor(tag + "gT", [128, NOC * TB], BF16))
            xr1t = es_f.enter_context(nc.sbuf_tensor(tag + "xr1t", [16, TB], BF16))
            xr3t = es_f.enter_context(nc.sbuf_tensor(tag + "xr3t", [16, TB], BF16))
            xr2t = es_f.enter_context(nc.sbuf_tensor(tag + "xr2t", [16, TB], BF16))
            xrtm = es_f.enter_context(nc.sbuf_tensor(tag + "xrtm", [128, 64], BF16))
            gT_t = gT.rearrange("p (oc t) -> p oc t", t=TB)

            for tb in range(NTB):
                t0 = tb * TB
                sfx = tag + str(tb)
                prep_a = fuse_prep and tb == 0
                prep_b = fuse_prep and tb == 0
                prep_q = fuse_prep and tb == 1
                with ExitStack() as es_a:
                    xnT = es_a.enter_context(
                        nc.sbuf_tensor(sfx + "xnT", [128, NDC * TB], BF16))
                    stats = es_a.enter_context(
                        nc.sbuf_tensor(sfx + "stats", [128, 8], F32))
                    slw1 = es_a.enter_context(
                        nc.sbuf_tensor(sfx + "slw1", [128, NDC * 384], BF16))
                    slw3 = es_a.enter_context(
                        nc.sbuf_tensor(sfx + "slw3", [128, NDC * 384], BF16))
                    s1b = es_a.enter_context(
                        nc.sbuf_tensor(sfx + "s1b", [128, 3 * 512], BF16))
                    bsl = es_a.enter_context(
                        nc.sbuf_tensor(sfx + "bsl", [16, 4 * 384], BF16))
                    trp_ = es_a.enter_context(
                        nc.psum_tensor(sfx + "trp", [128, 1024], BF16))
                    trp2_ = es_a.enter_context(
                        nc.psum_tensor(sfx + "trp2", [128, 1024], BF16))

                    banks = [trp_, trp2_]
                    bankfree = [P.snapw(), P.snapw()]
                    xnT_t = xnT.rearrange("p (dc t) -> p dc t", t=TB)
                    xr_free = [P.snapw()]

                    with ExitStack() as es_ln:
                        xbuf = es_ln.enter_context(
                            nc.sbuf_tensor(sfx + "xbuf", [128, D], F32))
                        xnb = es_ln.enter_context(
                            nc.sbuf_tensor(sfx + "xnb", [128, D], BF16))
                        xrp = es_ln.enter_context(
                            nc.psum_tensor(sfx + "xrp", [128, 512], F32))
                        xbuf_free = P.snapw()
                        xnb_free = P.snapw()
                        stats_free = P.snapw()
                        for tc in range(TC):
                            nx, nsq, nxn = emit_ln_tc(
                                P, src_v[:, tb * TC + tc, :], xbuf, xnb, stats,
                                EPS_AP, xbuf_free, xnb_free, stats_free)
                            lastpe = emit_transpose_fm(P, ident, banks, bankfree,
                                                       xnb, nxn, xnT_t, tc)
                            xbuf_free = [("c_dve", nxn), ("c_act", nsq)]
                            xnb_free = [("c_pe", lastpe)]
                            stats_free = [("c_dve", nxn)]
                        xnT_ready = ("c_dve", P.cnt["c_dve"])

                        def xr_pass(atile, outT, first_waits):
                            for tc in range(TC):
                                for dc in range(NDC):
                                    w = []
                                    if tc == 0 and dc == 0:
                                        w = list(first_waits) + xr_free[0]
                                    mm(P, xrp[:, tc * 16:(tc + 1) * 16],
                                       xnT_t[:, dc, tc * 128:(tc + 1) * 128],
                                       atile[:, dc * R:(dc + 1) * R],
                                       start=(dc == 0), stop=(dc == NDC - 1),
                                       waits=w,
                                       inc="c_pe" if (dc == NDC - 1) else None)
                            npe = P.cnt["c_pe"]
                            nev = cp_dve(P, xrtm[:, 0:64], xrp[:, 0:64],
                                         waits=[("c_pe", npe)] + xr_free[0])
                            lastpe = None
                            for tc in range(TC):
                                w = []
                                if tc == 0:
                                    w = [("c_dve", nev)] + bankfree[0]
                                lastpe = transpose(
                                    P, banks[0][0:16, tc * 128:(tc + 1) * 128],
                                    xrtm[:, tc * 16:(tc + 1) * 16], ident[:, :],
                                    waits=w, inc="c_pe" if tc == TC - 1 else None)
                            nev2 = cp_dve(P, outT[0:16, :], banks[0][0:16, 0:TB],
                                          waits=[("c_pe", lastpe)])
                            bankfree[0] = [("c_dve", nev2)]
                            xr_free[0] = [("c_dve", nev2), ("c_pe", lastpe)]
                            return ("c_dve", nev2)

                        xr1_ready = xr_pass(a1t, xr1t, [xnT_ready])
                        xr3_ready = xr_pass(a3t, xr3t, [xnT_ready])

                    h1ps = [es_a.enter_context(
                        nc.psum_tensor(sfx + f"h1ps{i}", [128, 512], F32))
                        for i in range(3)]
                    h3ps = [es_a.enter_context(
                        nc.psum_tensor(sfx + f"h3ps{i}", [128, 512], F32))
                        for i in range(3)]
                    natw = None
                    if prep_a:
                        natw = es_a.enter_context(
                            nc.sbuf_tensor(sfx + "natw", [128, 8192], BF16))

                    slabfree = {"w1": P.snapw(), "w3": P.snapw()}
                    slabready = {}
                    slabstored = {"w1": None, "w3": None}
                    natfree = [P.snapw()]
                    bslfree = [P.snapw(), P.snapw()]
                    bslready = {}
                    gmul_mark = {}
                    h1free = [P.snapw() for _ in range(3)]
                    h3free = [P.snapw() for _ in range(3)]
                    h1ready = [None] * 3
                    h3ready = [None] * 3
                    s1ready = [None] * 3
                    s1free = [P.snapw() for _ in range(3)]

                    def load_slab(mat, wi):
                        ow = OWS[wi]
                        base = D * OWB[wi]
                        srcf = w1t_s if mat == "w1" else w3t_s
                        buf = slw1 if mat == "w1" else slw3
                        sm = "s_sl1" if mat == "w1" else "s_sl3"
                        n = P.dma("sp", buf[:, 0:NDC * ow],
                                  srcf[base:base + D * ow].rearrange(
                                      "(p f) -> p f", p=128),
                                  waits=slabfree[mat], sem=sm)
                        slabready[(mat, wi)] = [(sm, n)]

                    natfree2 = [P.snapw(), P.snapw()]
                    wseq = [0]

                    def prep_window(mat, wi):
                        ow = OWS[wi]
                        nsub = ow // 128
                        wsrc = w1_v if mat == "w1" else w3_v
                        buf = slw1 if mat == "w1" else slw3
                        bufv = buf.rearrange("p (dc o) -> p dc o", o=ow)
                        smst = "s_psa" if mat == "w1" else "s_psb"
                        r0 = OWB[wi] // 128
                        nlds = {}

                        def issue_load(sub):
                            hi = (wseq[0] + sub) % 2
                            sm = "s_pna" if hi == 0 else "s_pnb"
                            n = P.dma("pool", natw[:, hi * D:(hi + 1) * D],
                                      wsrc[:, r0 + sub, :],
                                      waits=natfree2[hi], sem=sm)
                            nlds[sub] = (sm, n, hi)

                        issue_load(0)
                        if nsub > 1:
                            issue_load(1)
                        first_ev = [True, True]
                        gi = 0
                        for sub in range(nsub):
                            sm, nld, hi = nlds[sub]
                            for g0 in range(0, NDC, 8):
                                bi = gi % 2
                                bk = banks[bi]
                                lastpe = None
                                for j in range(8):
                                    dc = g0 + j
                                    w = []
                                    if j == 0:
                                        w = list(bankfree[bi])
                                        if g0 == 0:
                                            w.append((sm, nld))
                                    lastpe = transpose(
                                        P, bk[:, j * 128:(j + 1) * 128],
                                        natw[:, hi * D + dc * 128:
                                             hi * D + (dc + 1) * 128],
                                        ident[:, :],
                                        waits=w,
                                        inc="c_pe" if j == 7 else None)
                                wv_ = [("c_pe", lastpe)]
                                ei = gi % 2
                                if first_ev[ei]:
                                    wv_ += slabfree[mat]
                                    if slabstored[mat] is not None:
                                        wv_.append(slabstored[mat])
                                    first_ev[ei] = False
                                o_ap = bufv[:, g0:g0 + 8,
                                            sub * 128:(sub + 1) * 128]
                                i_ap = bk[:, 0:1024].rearrange(
                                    "p (g o) -> p g o", o=128)
                                if ei == 0:
                                    nev = cp_dve(P, o_ap, i_ap, waits=wv_)
                                    bankfree[bi] = [("c_dve", nev)]
                                else:
                                    nev = cp_act(P, o_ap, i_ap, waits=wv_)
                                    bankfree[bi] = [("c_act", nev)]
                                gi += 1
                            natfree2[(wseq[0] + sub) % 2] = \
                                [("c_pe", P.cnt["c_pe"])]
                            if sub + 2 < nsub:
                                issue_load(sub + 2)
                        wseq[0] += nsub
                        marks = [("c_dve", P.cnt["c_dve"]),
                                 ("c_act", P.cnt["c_act"])]
                        slabready[(mat, wi)] = marks
                        base = D * OWB[wi]
                        dstf = w1t_s if mat == "w1" else w3t_s
                        dview = dstf[base:base + D * ow].rearrange(
                            "(p f) -> p f", p=128)
                        nst = P.dma("sp", dview, buf[:, 0:NDC * ow],
                                    waits=marks, sem=smst)
                        slabstored[mat] = (smst, nst)

                    def load_bsl(wi):
                        ow = OWS[wi]
                        bo = (wi % 2) * 768
                        sm = "s_bsl" if wi % 2 == 0 else "s_bsl2"
                        P.dma("sp", bsl[0:16, bo:bo + ow],
                              b1t_s[0:R, OWB[wi]:OWB[wi] + ow],
                              waits=bslfree[wi % 2], sem=sm)
                        n3 = P.dma("sp", bsl[0:16, bo + 384:bo + 384 + ow],
                                   b3t_s[0:R, OWB[wi]:OWB[wi] + ow], sem=sm)
                        bslready[wi] = (sm, n3)

                    if prep_a:
                        prep_window("w1", 0)
                        load_bsl(0)
                        load_bsl(1)
                        prep_window("w3", 0)
                    else:
                        load_slab("w1", 0)
                        load_bsl(0)
                        load_bsl(1)
                        load_slab("w3", 0)

                    for wi in range(len(OWS)):
                        ow = OWS[wi]
                        nob = ow // 128
                        oc0 = OWB[wi] // 128
                        for ob in range(nob):
                            w = list(h1free[ob])
                            if ob == 0:
                                w += slabready[("w1", wi)]
                            for dc in range(NDC):
                                mm(P, h1ps[ob][:, 0:512],
                                   slw1[:, dc * ow + ob * 128:
                                        dc * ow + (ob + 1) * 128],
                                   xnT[:, dc * TB:(dc + 1) * TB],
                                   start=(dc == 0), stop=False,
                                   waits=w if dc == 0 else ())
                            n = mm(P, h1ps[ob][:, 0:512],
                                   bsl[0:16, (wi % 2) * 768 + ob * 128:
                                       (wi % 2) * 768 + (ob + 1) * 128],
                                   xr1t[0:16, :],
                                   start=False, stop=True,
                                   waits=([bslready[wi], xr1_ready]
                                          if ob == 0 else ()),
                                   inc="c_pe")
                            h1ready[ob] = ("c_pe", n)
                        slabfree["w1"] = [("c_pe", P.cnt["c_pe"])]
                        if wi + 1 < len(OWS):
                            if prep_a:
                                prep_window("w1", wi + 1)
                            else:
                                load_slab("w1", wi + 1)
                        for ob in range(nob):
                            n = P.eop("act",
                                      (lambda h, o=s1b[:, ob * 512:(ob + 1) * 512],
                                       i=h1ps[ob][:, 0:512]:
                                       h.activation(o, i, AF.Silu)),
                                      waits=[h1ready[ob]] + s1free[ob])
                            s1ready[ob] = ("c_act", n)
                            h1free[ob] = [("c_act", n)]
                        for ob in range(nob):
                            w = list(h3free[ob])
                            if ob == 0:
                                w += slabready[("w3", wi)]
                            for dc in range(NDC):
                                mm(P, h3ps[ob][:, 0:512],
                                   slw3[:, dc * ow + ob * 128:
                                        dc * ow + (ob + 1) * 128],
                                   xnT[:, dc * TB:(dc + 1) * TB],
                                   start=(dc == 0), stop=False,
                                   waits=w if dc == 0 else ())
                            n = mm(P, h3ps[ob][:, 0:512],
                                   bsl[0:16, (wi % 2) * 768 + 384 + ob * 128:
                                       (wi % 2) * 768 + 384 + (ob + 1) * 128],
                                   xr3t[0:16, :],
                                   start=False, stop=True,
                                   waits=([xr3_ready] if ob == 0 else ()),
                                   inc="c_pe")
                            h3ready[ob] = ("c_pe", n)
                        slabfree["w3"] = [("c_pe", P.cnt["c_pe"])]
                        bslfree[wi % 2] = [("c_pe", P.cnt["c_pe"])]
                        if wi + 1 < len(OWS):
                            if prep_a:
                                prep_window("w3", wi + 1)
                            else:
                                load_slab("w3", wi + 1)
                        if wi + 2 < len(OWS):
                            load_bsl(wi + 2)
                        for ob in range(nob):
                            n = P.eop("dve",
                                      (lambda h, o=gT_t[:, oc0 + ob, :],
                                       a=s1b[:, ob * 512:(ob + 1) * 512],
                                       i=h3ps[ob][:, 0:512]:
                                       h.tensor_tensor(o, a, i, OP.mult)),
                                      waits=[h3ready[ob], s1ready[ob]])
                            h3free[ob] = [("c_dve", n)]
                            s1free[ob] = [("c_dve", n)]
                            gmul_mark[oc0 + ob] = ("c_dve", n)
                    gT_ready = [("c_dve", P.cnt["c_dve"])]

                    for tc in range(TC):
                        for oc in range(NOC):
                            w = []
                            if tc == 0 and oc == 0:
                                w = list(gT_ready) + h1free[0]
                            mm(P, h1ps[0][:, tc * 16:(tc + 1) * 16],
                               gT_t[:, oc, tc * 128:(tc + 1) * 128],
                               a2t[:, oc * R:(oc + 1) * R],
                               start=(oc == 0), stop=(oc == NOC - 1),
                               waits=w, inc="c_pe" if oc == NOC - 1 else None)
                    nev = cp_dve(P, xrtm[:, 0:64], h1ps[0][:, 0:64],
                                 waits=[("c_pe", P.cnt["c_pe"])])
                    lastpe = None
                    for tc in range(TC):
                        w = []
                        if tc == 0:
                            w = [("c_dve", nev)] + bankfree[0]
                        lastpe = transpose(
                            P, banks[0][0:16, tc * 128:(tc + 1) * 128],
                            xrtm[:, tc * 16:(tc + 1) * 16], ident[:, :],
                            waits=w, inc="c_pe" if tc == TC - 1 else None)
                    nxr2 = cp_dve(P, xr2t[0:16, :], banks[0][0:16, 0:TB],
                                  waits=[("c_pe", lastpe)])
                    xr2_ready = ("c_dve", nxr2)

                with ExitStack() as es_b:
                    n_ops = TC if prep_b else 2 * TC
                    w2sl = [es_b.enter_context(
                        nc.sbuf_tensor(sfx + f"w2sl{i}", [128, 4 * 512], BF16))
                        for i in range(3)]
                    xres = [es_b.enter_context(
                        nc.sbuf_tensor(sfx + f"xres{i}", [128, TC * 512], F32))
                        for i in range(2)]
                    b2sl = es_b.enter_context(
                        nc.sbuf_tensor(sfx + "b2sl", [16, 2 * 512], BF16))
                    ops = [es_b.enter_context(
                        nc.psum_tensor(sfx + f"ops{i}", [128, 512], F32))
                        for i in range(n_ops)]
                    if prep_b:
                        natw2 = es_b.enter_context(
                            nc.sbuf_tensor(sfx + "natw2", [128, HID], BF16))
                        w2stg2 = es_b.enter_context(
                            nc.sbuf_tensor(sfx + "w2stg2", [128, HID], BF16))
                        trb = [es_b.enter_context(
                            nc.psum_tensor(sfx + f"trb{i}", [128, 1024], BF16))
                            for i in range(2)]
                        trbfree = [P.snapw(), P.snapw()]
                        natw2free = [P.snapw(), P.snapw()]
                        stg2free = [P.snapw()]
                        w2p_mark = {}
                    if prep_q:
                        natq = es_b.enter_context(
                            nc.sbuf_tensor(sfx + "natq", [128, 5120], BF16))
                        natqfree = [P.snapw()]

                    def prep_w2_chunk(dmc):
                        halves = [(0, 43), (43, 43)]
                        nldh = []
                        for hx, (o0, osz) in enumerate(halves):
                            hi = hx % 2
                            sm = "s_pna" if hi == 0 else "s_pnb"
                            n = P.dma("pool",
                                      natw2[:, hi * 5504:hi * 5504 + osz * 128],
                                      w2[dmc * 128:(dmc + 1) * 128,
                                         o0 * 128:(o0 + osz) * 128],
                                      waits=natw2free[hi], sem=sm)
                            nldh.append((sm, n))
                        first_ev = [True, True]
                        gidx = 0
                        for hx, (o0, osz) in enumerate(halves):
                            sm, nld = nldh[hx]
                            hi = hx % 2
                            for g0 in range(0, osz, 8):
                                gsz = min(8, osz - g0)
                                bi = gidx % 2
                                bk = trb[bi]
                                lastpe = None
                                for j in range(gsz):
                                    oc = o0 + g0 + j
                                    w = []
                                    if j == 0:
                                        w = list(trbfree[bi])
                                        if g0 == 0:
                                            w.append((sm, nld))
                                    lastpe = transpose(
                                        P, bk[:, j * 128:(j + 1) * 128],
                                        natw2[:, hi * 5504 + (g0 + j) * 128:
                                              hi * 5504 + (g0 + j + 1) * 128],
                                        ident[:, :],
                                        waits=w,
                                        inc="c_pe" if j == gsz - 1 else None)
                                ei = gidx % 2
                                wv_ = [("c_pe", lastpe)]
                                if first_ev[ei]:
                                    wv_ += stg2free[0]
                                    first_ev[ei] = False
                                oc_ap = w2stg2[:, (o0 + g0) * 128:
                                               (o0 + g0 + gsz) * 128]
                                if ei == 0:
                                    nev = cp_dve(P, oc_ap, bk[:, 0:gsz * 128],
                                                 waits=wv_)
                                    trbfree[bi] = [("c_dve", nev)]
                                else:
                                    nev = cp_act(P, oc_ap, bk[:, 0:gsz * 128],
                                                 waits=wv_)
                                    trbfree[bi] = [("c_act", nev)]
                                gidx += 1
                            natw2free[hi] = [("c_pe", P.cnt["c_pe"])]
                        smst = W2P_SEMS[dmc % 8]
                        nst = P.dma("sp", w2t_v2[:, :, dmc * 128:(dmc + 1) * 128],
                                    w2stg2[:, :].rearrange(
                                        "p (oc o) -> p oc o", o=128),
                                    waits=[("c_dve", P.cnt["c_dve"]),
                                           ("c_act", P.cnt["c_act"])], sem=smst)
                        w2p_mark[dmc] = (smst, nst)
                        stg2free[0] = [(smst, nst)]

                    def prep_wqe_chunk(dc):
                        nld = P.dma("pool", natq[:, 0:D],
                                    wq[dc * 128:(dc + 1) * 128, :],
                                    waits=natqfree[0], sem="s_pnb")
                        gv = natq[:, 0:D].rearrange("p (hk g dh) -> p hk g dh",
                                                    g=4, dh=128)
                        t1 = natq[:, D:D + KVD].rearrange(
                            "p (hk dh) -> p hk dh", dh=128)
                        t2 = natq[:, 0:KVD].rearrange(
                            "p (hk dh) -> p hk dh", dh=128)
                        out_v = natq[:, KVD:2 * KVD].rearrange(
                            "p (hk dh) -> p hk dh", dh=128)
                        P.eop("dve", (lambda h, o=t1, a=gv[:, :, 0, :],
                                      b_=gv[:, :, 1, :]: h.tensor_add(o, a, b_)),
                              waits=[("s_pnb", nld)])
                        P.eop("dve", (lambda h, o=t2, a=gv[:, :, 2, :],
                                      b_=gv[:, :, 3, :]: h.tensor_add(o, a, b_)))
                        nfold = P.eop("dve", (lambda h, o=out_v, a=t1, b_=t2:
                                              h.tensor_add(o, a, b_)))
                        nst = P.dma("sp", wqe_s[dc * 128:(dc + 1) * 128, :],
                                    natq[:, KVD:2 * KVD],
                                    waits=[("c_dve", nfold)], sem="s_psa")
                        natqfree[0] = [("s_psa", nst)]

                    ofree = [P.snapw() for _ in range(n_ops)]
                    xresfree = [P.snapw(), P.snapw()]
                    xres_ready = [None, None]
                    w2free = [P.snapw(), P.snapw(), P.snapw()]
                    w2ready = {}
                    w2seq = [0]
                    w2buf_of = {}
                    b2free = P.snapw()
                    b2ready = None

                    ocg_list = [(g0, min(4, NOC - g0)) for g0 in range(0, NOC, 4)]

                    W2SEMS = ("s_w2a", "s_w2b", "s_w2c")

                    def load_w2(dmw, gidx):
                        g0, gsz = ocg_list[gidx]
                        bi = w2seq[0] % 3
                        w2seq[0] += 1
                        w2buf_of[(dmw, gidx)] = bi
                        sm = W2SEMS[bi]
                        wl = list(w2free[bi])
                        if prep_b:
                            for r in range(4):
                                wl.append(w2p_mark[4 * dmw + r])
                        n = P.dma("sp",
                                  w2sl[bi][:, 0:gsz * 512].rearrange(
                                      "p (g o) -> p g o", o=512),
                                  w2t_v2[:, g0:g0 + gsz,
                                         dmw * 512:(dmw + 1) * 512],
                                  waits=wl, sem=sm)
                        w2ready[(dmw, gidx)] = (sm, n)

                    def load_xres(dmw):
                        bi = dmw % 2
                        sm = "s_xra" if bi == 0 else "s_xrb"
                        n = P.dma("sp",
                                  xres[bi][:, :].rearrange(
                                      "p (tt o) -> p tt o", o=512),
                                  src_v[:, tb * TC:(tb + 1) * TC,
                                        dmw * 512:(dmw + 1) * 512],
                                  waits=xresfree[bi], sem=sm)
                        xres_ready[bi] = (sm, n)

                    if prep_b:
                        for dmc in range(4):
                            prep_w2_chunk(dmc)
                    nb2 = P.dma("sp", b2sl[0:16, 0:512], b2t_s[0:R, 0:512],
                                waits=b2free, sem="s_b2")
                    b2ready = {0: ("s_b2", nb2)}
                    b2_lora_done = [0, 0]

                    def load_b2(dmw2):
                        pb2 = dmw2 % 2
                        sm = "s_b2" if pb2 == 0 else "s_b22"
                        n = P.dma("sp",
                                  b2sl[0:16, pb2 * 512:pb2 * 512 + 512],
                                  b2t_s[0:R, dmw2 * 512:(dmw2 + 1) * 512],
                                  waits=[("c_pe", b2_lora_done[pb2])],
                                  sem=sm)
                        b2ready[dmw2] = (sm, n)

                    load_w2(0, 0)
                    load_xres(0)

                    load_w2(0, 1)
                    for dmw in range(NDMW):
                        bi = dmw % 2
                        ob = 0 if prep_b else (dmw % 2) * TC
                        for gidx, (g0, gsz) in enumerate(ocg_list):
                            if gidx + 2 < len(ocg_list):
                                load_w2(dmw, gidx + 2)
                            elif dmw + 1 < NDMW:
                                load_w2(dmw + 1, gidx + 2 - len(ocg_list))
                                if gidx + 2 == len(ocg_list):
                                    load_b2(dmw + 1)
                            if prep_b and dmw + 1 < NDMW and \
                                    gidx in (4, 9, 14, 19):
                                prep_w2_chunk(4 * (dmw + 1) + (4, 9, 14, 19).index(gidx))
                            if prep_q and gidx in (4, 9, 14, 19):
                                prep_wqe_chunk(4 * dmw + (4, 9, 14, 19).index(gidx))
                            wsl = w2sl[w2buf_of[(dmw, gidx)]]
                            for j in range(gsz):
                                oc = g0 + j
                                for tc in range(TC):
                                    w = []
                                    if j == 0 and tc == 0:
                                        w.append(w2ready[(dmw, gidx)])
                                        w.append(gmul_mark[g0 + gsz - 1])
                                    if gidx == 0 and j == 0:
                                        w = w + ofree[ob + tc]
                                    mm(P, ops[ob + tc][:, 0:512],
                                       gT_t[:, oc, tc * 128:(tc + 1) * 128],
                                       wsl[:, j * 512:(j + 1) * 512],
                                       start=(oc == 0), stop=False,
                                       waits=w,
                                       inc="c_pe" if (j == gsz - 1 and
                                                      tc == TC - 1) else None)
                            w2free[w2buf_of[(dmw, gidx)]] = \
                                [("c_pe", P.cnt["c_pe"])]
                        oready = [None] * TC
                        for tc in range(TC):
                            w = []
                            if tc == 0:
                                w = [b2ready[dmw], xr2_ready]
                            n = mm(P, ops[ob + tc][:, 0:512],
                                   xr2t[0:16, tc * 128:(tc + 1) * 128],
                                   b2sl[0:16, (dmw % 2) * 512:
                                        (dmw % 2) * 512 + 512],
                                   start=False, stop=True, waits=w, inc="c_pe")
                            oready[tc] = ("c_pe", n)
                        b2_lora_done[dmw % 2] = P.cnt["c_pe"]
                        for tc in range(TC):
                            n = P.eop("dve",
                                      (lambda h,
                                       o=xres[bi][:, tc * 512:(tc + 1) * 512],
                                       i=ops[ob + tc][:, 0:512],
                                       x_=xres[bi][:, tc * 512:(tc + 1) * 512]:
                                       h.scalar_tensor_tensor(
                                           o, i, 0.5, x_, OP.mult, OP.add)),
                                      waits=[oready[tc], xres_ready[bi]])
                            ofree[ob + tc] = [("c_dve", n)]
                        smst = "s_sta" if bi == 0 else "s_stb"
                        nst = P.dma("sp",
                                    dst_v[:, tb * TC:(tb + 1) * TC,
                                          dmw * 512:(dmw + 1) * 512],
                                    xres[bi][:, :].rearrange(
                                        "p (tt o) -> p tt o", o=512),
                                    waits=[("c_dve", P.cnt["c_dve"])], sem=smst)
                        xresfree[bi] = [(smst, nst)]
                        if dmw + 1 < NDMW:
                            load_xres(dmw + 1)
                    if prep_q:
                        pass
            P.barrier()
            P.play()

    ffn(x_in, x1h, "f1", fuse_prep=True)

    with ExitStack() as es2:
        Qfm = es2.enter_context(nc.sbuf_tensor("Qfm", [128, NH * TOK], BF16))
        VALfm = es2.enter_context(nc.sbuf_tensor("VALfm", [128, NH * TOK], BF16))

        with ExitStack() as es_a:
            xnT = es_a.enter_context(nc.sbuf_tensor("a_xnT", [128, NDC * TB], BF16))
            xbuf = es_a.enter_context(nc.sbuf_tensor("a_xbuf", [128, D], F32))
            xnb = es_a.enter_context(nc.sbuf_tensor("a_xnb", [128, D], BF16))
            stats = es_a.enter_context(nc.sbuf_tensor("a_stats", [128, 8], F32))
            wqsl = [es_a.enter_context(
                nc.sbuf_tensor("a_wqsl0", [128, NDC * 128], BF16))]
            wqsl.append(wqsl[0])
            kvw_b = [es_a.enter_context(
                nc.sbuf_tensor(f"a_kvw{i}", [128, NDC * 512], BF16))
                for i in range(2)]
            kvout = es_a.enter_context(
                nc.sbuf_tensor("a_kvout", [128, TC * 512], BF16))
            kqps = [es_a.enter_context(
                nc.psum_tensor(f"a_kq{i}", [128, 512], F32)) for i in range(TC)]
            trp_ = es_a.enter_context(nc.psum_tensor("a_trp", [128, 1024], BF16))
            trp2_ = es_a.enter_context(nc.psum_tensor("a_trp2", [128, 1024], BF16))

            banks = [trp_, trp2_]
            bankfree = [P.snapw(), P.snapw()]
            xnT_t = xnT.rearrange("p (dc t) -> p dc t", t=TB)

            wqe_v = wqe_s.rearrange("(dc p) f -> p dc f", p=128)
            wk_v = wk.rearrange("(dc p) f -> p dc f", p=128)
            wv_v = wv.rearrange("(dc p) f -> p dc f", p=128)
            kxo_v = kx_own.rearrange("(tt p) f -> p tt f", p=128)
            x1_v = x1h.rearrange("(tt p) d -> p tt d", p=128)

            wq_free0 = [P.snapw()]
            wq_free = {0: None, 1: None}

            class _WQF:
                def __getitem__(self, i):
                    return wq_free0[0]

                def __setitem__(self, i, v):
                    wq_free0[0] = v
            wq_free = _WQF()
            kv_free = [P.snapw(), P.snapw()]
            kvo_free = P.snapw()
            pfree = [P.snapw() for _ in range(TC)]
            xbuf_free = P.snapw()
            xnb_free = P.snapw()
            stats_free = P.snapw()

            for tb in range(NTB):
                t0 = tb * TB
                for tc in range(TC):
                    nx, nsq, nxn = emit_ln_tc(
                        P, x1_v[:, tb * TC + tc, :], xbuf, xnb, stats,
                        EPS_AP, xbuf_free, xnb_free, stats_free)
                    lastpe = emit_transpose_fm(P, ident, banks, bankfree,
                                               xnb, nxn, xnT_t, tc)
                    xbuf_free = [("c_dve", nxn), ("c_act", nsq)]
                    xnb_free = [("c_pe", lastpe)]
                    stats_free = [("c_dve", nxn)]
                xnT_ready = ("c_dve", P.cnt["c_dve"])

                for fc in range(NH):
                    bi = fc % 2
                    smq = "s_wqa" if bi == 0 else "s_wqb"
                    nwq = P.dma("sp",
                                wqsl[bi][:, :].rearrange(
                                    "p (dc f) -> p dc f", f=128),
                                wqe_v[:, :, fc * 128:(fc + 1) * 128],
                                waits=wq_free[bi], sem=smq)
                    tcq = fc % TC
                    lastpe = None
                    for dc in range(NDC):
                        w = []
                        if dc == 0:
                            w = [(smq, nwq), xnT_ready] + pfree[tcq]
                        lastpe = mm(P, kqps[tcq][:, 0:512],
                                    wqsl[bi][:, dc * 128:(dc + 1) * 128],
                                    xnT_t[:, dc, :],
                                    start=(dc == 0), stop=(dc == NDC - 1),
                                    waits=w,
                                    inc="c_pe" if dc == NDC - 1 else None)
                    wq_free[bi] = [("c_pe", lastpe)]
                    nev = cp_dve(P, Qfm[:, fc * TOK + t0:fc * TOK + t0 + TB],
                                 kqps[tcq][:, 0:512], waits=[("c_pe", lastpe)])
                    pfree[tcq] = [("c_dve", nev)]

                kv_jobs = [(wk_v, 0, 0), (wk_v, 0, 1), (wv_v, KVD, 0),
                           (wv_v, KVD, 1)]

                def load_kv(ji):
                    wsrc, colbase, kvw = kv_jobs[ji]
                    bi = ji % 2
                    sm = "s_kv" if bi == 0 else "s_wo"
                    n = P.dma("pool",
                              kvw_b[bi][:, :].rearrange(
                                  "p (dc f) -> p dc f", f=512),
                              wsrc[:, :, kvw * 512:(kvw + 1) * 512],
                              waits=kv_free[bi], sem=sm)
                    kv_ready[ji] = (sm, n)

                kv_ready = {}
                load_kv(0)
                load_kv(1)
                for ji, (wsrc, colbase, kvw) in enumerate(kv_jobs):
                    if True:
                        bi = ji % 2
                        oready = [None] * TC
                        for tc in range(TC):
                            lastpe = None
                            for dc in range(NDC):
                                w = []
                                if dc == 0:
                                    w = [kv_ready[ji], xnT_ready] + pfree[tc]
                                lastpe = mm(
                                    P, kqps[tc][:, 0:512],
                                    xnT_t[:, dc, tc * 128:(tc + 1) * 128],
                                    kvw_b[bi][:, dc * 512:(dc + 1) * 512],
                                    start=(dc == 0), stop=(dc == NDC - 1),
                                    waits=w,
                                    inc="c_pe" if dc == NDC - 1 else None)
                            oready[tc] = ("c_pe", lastpe)
                        kv_free[bi] = [("c_pe", P.cnt["c_pe"])]
                        if ji + 2 < 4:
                            load_kv(ji + 2)
                        for tc in range(TC):
                            w = [oready[tc]]
                            if tc == 0:
                                w += kvo_free
                            nev = cp_dve(P, kvout[:, tc * 512:(tc + 1) * 512],
                                         kqps[tc][:, 0:512], waits=w)
                            pfree[tc] = [("c_dve", nev)]
                        nst = P.dma("sp",
                                    kxo_v[:, tb * TC:(tb + 1) * TC,
                                          colbase + kvw * 512:
                                          colbase + (kvw + 1) * 512],
                                    kvout[:, :].rearrange(
                                        "p (tt o) -> p tt o", o=512),
                                    waits=[("c_dve", P.cnt["c_dve"])],
                                    sem="s_kst")
                        kvo_free = [("s_kst", nst)]

                P.wait("pool", "s_kst", P.cnt["s_kst"])
                P.wait("pool", "cc", P.cnt["cc"])

                def cc_fn(h, tb=tb):
                    return h.collective_compute(
                        "AllGather", OP.bypass,
                        replica_groups=[[0, 1], [2, 3], [4, 5], [6, 7]],
                        ins=[kx_own[tb * TB:(tb + 1) * TB, :]],
                        outs=[kx_all[tb * 2 * TB:(tb + 1) * 2 * TB, :]])
                P.op("pool", cc_fn, inc="cc")

            P.barrier()
            P.play()

        with ExitStack() as es_d:
            Kfm = es_d.enter_context(nc.sbuf_tensor("Kfm", [128, NH * S], BF16))
            ktmp = [es_d.enter_context(
                nc.sbuf_tensor(f"d_ktmp{i}", [128, KVD], BF16))
                for i in range(2)]
            Vh = [es_d.enter_context(
                nc.sbuf_tensor(f"d_Vh{i}", [128, S], BF16)) for i in range(2)]
            Pb = [es_d.enter_context(
                nc.sbuf_tensor(f"d_P{i}", [128, S], BF16)) for i in range(2)]
            PT = es_d.enter_context(nc.sbuf_tensor("d_PT", [128, NKC * 512], BF16))
            sums = es_d.enter_context(nc.sbuf_tensor("d_sums", [128, 8], F32))
            scps = es_d.enter_context(nc.psum_tensor("d_sc", [128, 2048], F32))
            pvps = es_d.enter_context(nc.psum_tensor("d_pv", [128, 512], F32))
            trp_ = es_d.enter_context(nc.psum_tensor("d_trp", [128, 1024], BF16))
            trp2_ = es_d.enter_context(nc.psum_tensor("d_trp2", [128, 1024], BF16))

            banks = [trp_, trp2_]
            bankfree = [P.snapw(), P.snapw()]
            NRND = max(1, NKB // 4)
            RW = min(NKB, 4) * 512

            cc_base = P.cnt["cc"] - NTB
            Kfm_t = Kfm.rearrange("p (hh k) -> p hh k", k=S)
            ktfree = [P.snapw(), P.snapw()]
            ktld = {}
            NKC_TB = NKC // NTB

            def load_kt(kc):
                tbi = kc % 2
                sm = "s_kt" if tbi == 0 else "s_vh"
                n = P.dma("sp", ktmp[tbi][:, :],
                          kx_all[kc * 128:(kc + 1) * 128, 0:KVD],
                          waits=ktfree[tbi] + [("cc", cc_base + kc // NKC_TB + 1)],
                          sem=sm)
                ktld[kc] = (sm, n)

            load_kt(0)
            load_kt(1)
            for kc in range(NKC):
                tbi = kc % 2
                bi = kc % 2
                bk = banks[bi]
                lastpe = None
                for hh in range(NH):
                    w = []
                    if hh == 0:
                        w = [ktld[kc]] + bankfree[bi]
                    lastpe = transpose(P, bk[:, hh * 128:(hh + 1) * 128],
                                       ktmp[tbi][:, hh * 128:(hh + 1) * 128],
                                       ident[:, :],
                                       waits=w, inc="c_pe" if hh == NH - 1 else None)
                ktfree[tbi] = [("c_pe", lastpe)]
                if kc + 2 < NKC:
                    load_kt(kc + 2)
                nev = cp_dve(P, Kfm_t[:, :, kc * 128:(kc + 1) * 128],
                             bk[:, :].rearrange("p (j o) -> p j o", o=128),
                             waits=[("c_pe", lastpe)])
                bankfree[bi] = [("c_dve", nev)]
            kfm_ready = ("c_dve", P.cnt["c_dve"])

            kx_v = kx_all.rearrange("(rk p) f -> p rk f", p=128)
            PT_t = PT.rearrange("p (kc q) -> p kc q", q=512)

            scfree = [kfm_ready] + P.snapw()
            pvfree = P.snapw()
            vh_free = [P.snapw(), P.snapw()]
            vh_ready = {}

            def load_vh(hh):
                vbi = hh % 2
                sm = "s_vh" if vbi == 0 else "s_kt"
                n = None
                for tbx in range(NTB):
                    n = P.dma("sp",
                              Vh[vbi][:, tbx * NKC_TB * 128:
                                      (tbx + 1) * NKC_TB * 128].rearrange(
                                  "p (kc o) -> p kc o", o=128),
                              kx_v[:, tbx * NKC_TB:(tbx + 1) * NKC_TB,
                                   KVD + hh * 128:KVD + (hh + 1) * 128],
                              waits=(vh_free[vbi] if tbx == 0 else []) +
                                    [("cc", cc_base + tbx + 1)],
                              sem=sm)
                vh_ready[hh] = (sm, n)
            sums_free = [P.snapw(), P.snapw()]
            pbfree = [P.snapw(), P.snapw()]
            ptfree = [P.snapw()]

            def emit_tail(qt, nexp_marks):
                par = qt % 2
                nred = P.eop("dve", (lambda h, o=sums[:, par * 4 + 2:par * 4 + 3],
                                     i=sums[:, par * 4:par * 4 + NRND]:
                                     h.reduce_sum(o, i, axis=AX)),
                             waits=nexp_marks)
                nrec = P.eop("dve", (lambda h, o=sums[:, par * 4 + 3:par * 4 + 4],
                                     i=sums[:, par * 4 + 2:par * 4 + 3]:
                                     h.reciprocal(o, i)))
                sums_free[par] = [("c_dve", nrec)]
                nnorm = P.eop("dve", (lambda h, o=Pb[par][:, :], i=Pb[par][:, :],
                                      s_=sums[:, par * 4 + 3:par * 4 + 4]:
                                      h.tensor_scalar_mul(o, i, s_)))
                lastpe = None
                for gidx, g0 in enumerate(range(0, NKC, 8)):
                    bi = gidx % 2
                    bk = banks[bi]
                    for j in range(8):
                        kc = g0 + j
                        w = []
                        if j == 0:
                            w = [("c_dve", nnorm)] + bankfree[bi]
                        lastpe = transpose(
                            P, bk[:, j * 128:(j + 1) * 128],
                            Pb[par][:, kc * 128:(kc + 1) * 128], ident[:, :],
                            waits=w, inc="c_pe" if j == 7 else None)
                    wv_ = [("c_pe", lastpe)]
                    if gidx < 2:
                        wv_ += ptfree[0]
                    nev = cp_dve(P, PT_t[:, g0:g0 + 8,
                                         qt * 128:(qt + 1) * 128],
                                 bk[:, :].rearrange("p (j o) -> p j o", o=128),
                                 waits=wv_)
                    bankfree[bi] = [("c_dve", nev)]
                pbfree[par] = [("c_pe", lastpe)]

            load_vh(0)
            load_vh(1)
            for hh in range(NH):
                vbi = hh % 2
                for qg in range(TOK // 512):
                    tails = []
                    for qt in range(TC):
                        par = qt % 2
                        q0 = qg * 512 + qt * 128
                        nexp_marks = []
                        for rnd in range(NRND):
                            lastmm = None
                            for j in range(RW // 512):
                                kb = rnd * 4 + j
                                w = []
                                if j == 0:
                                    w = list(scfree)
                                lastmm = mm(
                                    P, scps[:, j * 512:(j + 1) * 512],
                                    Qfm[:, hh * TOK + q0:hh * TOK + q0 + 128],
                                    Kfm_t[:, hh, kb * 512:(kb + 1) * 512],
                                    start=True, stop=True,
                                    waits=w,
                                    inc="c_pe" if j == RW // 512 - 1 else None)
                            wexp = [("c_pe", lastmm)] + sums_free[par]
                            if rnd == 0:
                                wexp += pbfree[par]
                            nex = P.eop("act",
                                        (lambda h,
                                         o=Pb[par][:, rnd * RW:(rnd + 1) * RW],
                                         i=scps[:, 0:RW],
                                         b_=SMB_AP, s_=SMS_AP,
                                         a=sums[:, par * 4 + rnd:
                                                par * 4 + rnd + 1]:
                                         h.activation(o, i, AF.Exp, bias=b_,
                                                      scale=s_, accum_out=a)),
                                        waits=wexp)
                            nexp_marks.append(("c_act", nex))
                            scfree = [("c_act", nex)]
                        tails.append((qt, nexp_marks))
                        if len(tails) > 1:
                            emit_tail(*tails.pop(0))
                    emit_tail(*tails.pop(0))
                    pt_ready = ("c_dve", P.cnt["c_dve"])
                    lastpe = None
                    for kc in range(NKC):
                        w = []
                        if kc == 0:
                            w = [pt_ready, vh_ready[hh]] + pvfree
                        lastpe = mm(P, pvps[:, 0:512],
                                    Vh[vbi][:, kc * 128:(kc + 1) * 128],
                                    PT_t[:, kc, :],
                                    start=(kc == 0), stop=(kc == NKC - 1),
                                    waits=w,
                                    inc="c_pe" if kc == NKC - 1 else None)
                    ptfree[0] = [("c_pe", lastpe)]
                    nev = cp_dve(P, VALfm[:, hh * TOK + qg * 512:
                                          hh * TOK + (qg + 1) * 512],
                                 pvps[:, 0:512], waits=[("c_pe", lastpe)])
                    pvfree = [("c_dve", nev)]
                vh_free[vbi] = [("c_pe", P.cnt["c_pe"])]
                if hh + 2 < NH:
                    load_vh(hh + 2)
            val_ready = ("c_dve", P.cnt["c_dve"])
            P.barrier()
            P.play()

        with ExitStack() as es_e:
            wob = [es_e.enter_context(
                nc.sbuf_tensor(f"e_wob{i}", [128, NH * 512], BF16))
                for i in range(2)]
            xres = [es_e.enter_context(
                nc.sbuf_tensor(f"e_xres{i}", [128, TC * 512], F32))
                for i in range(2)]
            ops = [es_e.enter_context(
                nc.psum_tensor(f"e_ops{i}", [128, 512], F32)) for i in range(4)]

            wo_v = wo.rearrange("(kc p) d -> p kc d", p=128)
            x1_v2 = x1h.rearrange("(tt p) d -> p tt d", p=128)
            x2_v = x2h.rearrange("(tt p) d -> p tt d", p=128)

            wofree = [P.snapw(), P.snapw()]
            ofree = [P.snapw() for _ in range(4)]
            xresfree = [P.snapw(), P.snapw()]
            wo_jobs = [(tg, dmw) for tg in range(TOK // 512)
                       for dmw in range(NDMW)]
            wo_ready = {}

            def load_wo(ji):
                _, dmw = wo_jobs[ji]
                bi = ji % 2
                sm = "s_wo" if bi == 0 else "s_kv"
                n = P.dma("pool",
                          wob[bi][:, :].rearrange("p (kc o) -> p kc o", o=512),
                          wo_v[:, :, dmw * 512:(dmw + 1) * 512],
                          waits=wofree[bi], sem=sm)
                wo_ready[ji] = (sm, n)

            load_wo(0)
            load_wo(1)
            for ji, (tg, dmw) in enumerate(wo_jobs):
                if True:
                    wbi = ji % 2
                    woready = wo_ready[ji]
                    bi = dmw % 2
                    smx = "s_xra" if bi == 0 else "s_xrb"
                    nxr = P.dma("sp",
                                xres[bi][:, :].rearrange(
                                    "p (tt o) -> p tt o", o=512),
                                x1_v2[:, tg * TC:(tg + 1) * TC,
                                      dmw * 512:(dmw + 1) * 512],
                                waits=xresfree[bi], sem=smx)
                    oready = [None] * 4
                    for tt in range(4):
                        lastpe = None
                        for kvc in range(NH):
                            w = []
                            if kvc == 0:
                                w = [woready, val_ready] + ofree[tt]
                            lastpe = mm(
                                P, ops[tt][:, 0:512],
                                VALfm[:, kvc * TOK + tg * 512 + tt * 128:
                                      kvc * TOK + tg * 512 + (tt + 1) * 128],
                                wob[wbi][:, kvc * 512:(kvc + 1) * 512],
                                start=(kvc == 0), stop=(kvc == NH - 1),
                                waits=w, inc="c_pe" if kvc == NH - 1 else None)
                        oready[tt] = ("c_pe", lastpe)
                    wofree[wbi] = [("c_pe", P.cnt["c_pe"])]
                    if ji + 2 < len(wo_jobs):
                        load_wo(ji + 2)
                    for tt in range(4):
                        n = P.eop("dve",
                                  (lambda h,
                                   o=xres[bi][:, tt * 512:(tt + 1) * 512],
                                   i=ops[tt][:, 0:512],
                                   x_=xres[bi][:, tt * 512:(tt + 1) * 512]:
                                   h.tensor_tensor(o, i, x_, OP.add)),
                                  waits=[oready[tt], (smx, nxr)])
                        ofree[tt] = [("c_dve", n)]
                    smst = "s_sta" if bi == 0 else "s_stb"
                    nst = P.dma("sp",
                                x2_v[:, tg * TC:(tg + 1) * TC,
                                     dmw * 512:(dmw + 1) * 512],
                                xres[bi][:, :].rearrange(
                                    "p (tt o) -> p tt o", o=512),
                                waits=[("c_dve", P.cnt["c_dve"])], sem=smst)
                    xresfree[bi] = [(smst, nst)]
            P.barrier()
            P.play()

    ffn(x2h, y_out, "f3")

    for smn in SNAP_SEMS:
        P.wait("sp", smn, P.cnt[smn])
    P.play()

    es_top.close()
    return nc



def kernel(L0_attn_x, L0_attn_wq, L0_attn_wk, L0_attn_wv, L0_attn_wo,
           L0_ffn_w1, L0_ffn_w3, L0_ffn_w2,
           L0_ffn_lora_A1, L0_ffn_lora_B1, L0_ffn_lora_A3, L0_ffn_lora_B3,
           L0_ffn_lora_A2, L0_ffn_lora_B2):
    x = np.ascontiguousarray(np.asarray(L0_attn_x, dtype=np.float32))
    x2d = x.reshape(B * S, D)
    ident = np.eye(128, dtype=ml_dtypes.bfloat16)
    consts = np.tile(np.array([[EPS, SM_BIAS, SM_SCALE, 0.0]], dtype=np.float32),
                     (128, 1))
    common = {
        "wq": np.ascontiguousarray(np.asarray(L0_attn_wq, dtype=np.float32)),
        "wk": np.ascontiguousarray(np.asarray(L0_attn_wk, dtype=np.float32)),
        "wv": np.ascontiguousarray(np.asarray(L0_attn_wv, dtype=np.float32)),
        "wo": np.ascontiguousarray(np.asarray(L0_attn_wo, dtype=np.float32)),
        "w1": np.ascontiguousarray(np.asarray(L0_ffn_w1, dtype=np.float32)),
        "w3": np.ascontiguousarray(np.asarray(L0_ffn_w3, dtype=np.float32)),
        "w2": np.ascontiguousarray(np.asarray(L0_ffn_w2, dtype=np.float32)),
        "a1": np.ascontiguousarray(np.asarray(L0_ffn_lora_A1, dtype=np.float32)),
        "b1": np.ascontiguousarray(np.asarray(L0_ffn_lora_B1, dtype=np.float32)),
        "a3": np.ascontiguousarray(np.asarray(L0_ffn_lora_A3, dtype=np.float32)),
        "b3": np.ascontiguousarray(np.asarray(L0_ffn_lora_B3, dtype=np.float32)),
        "a2": np.ascontiguousarray(np.asarray(L0_ffn_lora_A2, dtype=np.float32)),
        "b2": np.ascontiguousarray(np.asarray(L0_ffn_lora_B2, dtype=np.float32)),
        "identw": ident,
        "consts": consts,
    }
    in_maps = []
    for c in range(NCORE):
        m = dict(common)
        m["x_in"] = np.ascontiguousarray(x2d[c * TOK:(c + 1) * TOK, :])
        in_maps.append(m)

    nc = build_program()
    res = run_bass_kernel_spmd(nc, in_maps, core_ids=list(range(NCORE)))
    y = np.concatenate([res.results[c]["y_out"] for c in range(NCORE)], axis=0)
    return y.reshape(B, S, D)


if __name__ == "__main__":
    import time
    t0 = time.time()
    nc = build_program()
    print(f"build ok in {time.time()-t0:.1f}s")
